# version 18
# baseline (speedup 1.0000x reference)
import contextlib
import numpy as np
import ml_dtypes
import concourse.bass as bass
import concourse.mybir as mybir
from concourse.bass_utils import run_bass_kernel_spmd

F32 = mybir.dt.float32
BF16 = mybir.dt.bfloat16
I8 = mybir.dt.int8
I32 = mybir.dt.int32
AF = mybir.ActivationFunctionType
ALU = mybir.AluOpType
AX = mybir.AxisListType

NCORES = 8
B = 4
TC = 512
NT = 4
ND = 8
C = 1024
DEBUG_DUMPS = False
RSQRT_MAGIC = 0x5F3759DF
RCP_MAGIC = 0x7EF311C3


class Sched:

    def __init__(self):
        self.ops = {k: [] for k in ("sync", "scalar", "vector", "tensor", "gpsimd")}
        self.cnt = {}

    AUTO = {"vector": "DV", "scalar": "AC", "gpsimd": "GP"}

    def emit(self, eng, fn, waits=(), inc=None):
        waits = list(waits)
        if fn is not None and eng in self.AUTO and inc is None:
            seq = self.AUTO[eng]
            prev = self.cnt.get(seq, 0)
            if prev > 0:
                waits.append((seq, prev))
            inc = (seq, 1)
        if inc is not None:
            self.cnt[inc[0]] = self.cnt.get(inc[0], 0) + inc[1]
        self.ops[eng].append((fn, tuple(waits), inc))
        return self.cnt.get(inc[0]) if inc is not None else None

    def check(self):
        pos = {k: 0 for k in self.ops}
        sem = {}
        progress = True
        while progress:
            progress = False
            for eng, lst in self.ops.items():
                while pos[eng] < len(lst):
                    fn, waits, inc = lst[pos[eng]]
                    if all(sem.get(s, 0) >= (v() if callable(v) else v)
                           for s, v in waits):
                        if inc is not None:
                            sem[inc[0]] = sem.get(inc[0], 0) + inc[1]
                        pos[eng] += 1
                        progress = True
                    else:
                        break
        stuck = {e: p for e, p in pos.items() if p < len(self.ops[e])}
        if stuck:
            msgs = []
            for e, p in stuck.items():
                _, waits, _ = self.ops[e][p]
                unmet = [(s, (v() if callable(v) else v), sem.get(s, 0)) for s, v in waits
                         if sem.get(s, 0) < (v() if callable(v) else v)]
                msgs.append(f"{e} op {p}/{len(self.ops[e])} waits {unmet}")
            raise RuntimeError("schedule deadlock: " + "; ".join(msgs))


def build_nc():
    nc = bass.Bass("TRN2", target_bir_lowering=False,
                   detect_race_conditions=False)

    d_xfm = nc.dram_tensor("xfm", [B, C, TC], F32, kind="ExternalInput").ap()
    d_xtok = nc.dram_tensor("xtok", [B, TC, C], F32, kind="ExternalInput").ap()
    d_wvT = nc.dram_tensor("wvT", [C, C], F32, kind="ExternalInput").ap()
    d_wrT = nc.dram_tensor("wrT", [C, C], F32, kind="ExternalInput").ap()
    d_woT = nc.dram_tensor("woT", [C, C], F32, kind="ExternalInput").ap()
    d_pkT = nc.dram_tensor("pkT", [1024, 256], I32, kind="ExternalInput").ap()
    d_tdt = nc.dram_tensor("tdt", [128, ND], F32, kind="ExternalInput").ap()
    d_tft = nc.dram_tensor("tft", [128, ND], F32, kind="ExternalInput").ap()
    d_iota = nc.dram_tensor("iota", [128, TC], F32, kind="ExternalInput").ap()
    d_mask = nc.dram_tensor("maskt", [128, 8], F32, kind="ExternalInput").ap()
    d_gsel = nc.dram_tensor("gsel", [128, 64], F32, kind="ExternalInput").ap()
    d_gselc = nc.dram_tensor("gselc", [128, 64], F32, kind="ExternalInput").ap()
    d_wscol = nc.dram_tensor("wscol", [128, 1], F32, kind="ExternalInput").ap()
    d_ident = nc.dram_tensor("ident", [128, 128], BF16, kind="ExternalInput").ap()
    d_biast = nc.dram_tensor("biast", [128, C], F32, kind="ExternalInput").ap()
    d_y = nc.dram_tensor("y", [B, TC, C], F32, kind="ExternalOutput").ap()
    d_ccin = nc.dram_tensor("ccin", [B, C], F32, kind="Internal").ap()
    d_ccout = nc.dram_tensor("ccout", [B, 8, C], F32, kind="Internal").ap()

    es = contextlib.ExitStack()

    def sbuf(name, shape, dt):
        return es.enter_context(nc.sbuf_tensor(name, shape, dt))

    def psum(name, shape, dt):
        return es.enter_context(nc.psum_tensor(name, shape, dt))

    with es:
        wv16 = sbuf("wv16", [128, ND * C], BF16)
        wr16 = sbuf("wr16", [128, ND * C], BF16)
        wo16 = sbuf("wo16", [128, ND * C], BF16)
        wt16 = sbuf("wt16", [128, ND * C], BF16)
        rcpd = sbuf("rcpd", [128, ND * TC], F32)
        denb = sbuf("denb", [128, C], F32)
        powb = sbuf("powb", [128, ND * TC], F32)
        sf32 = sbuf("sf32", [128, C], F32)
        xqi8 = sbuf("xqi8", [128, NT * C], I8)
        idn = sbuf("idn", [128, 128], BF16)
        xfm16 = sbuf("xfm16", [128, 2 * ND * TC], BF16)
        xqT16 = sbuf("xqT16", [128, ND * TC], BF16)
        scn = sbuf("scn", [128, 2 * TC], F32)
        num = sbuf("num", [128, 2 * ND * TC], F32)
        rw16 = sbuf("rw16", [128, ND * TC], BF16)
        y1b = sbuf("y1b", [128, NT * C], F32)
        sint = y1b[:, 0:TC].bitcast(I32)
        outb = sbuf("outb", [128, C], F32)
        iot = outb[:, 0:TC]
        tdq = sbuf("tdq", [128, ND], F32)
        tfq = sbuf("tfq", [128, ND], F32)
        mneg = sbuf("mneg", [128, ND], F32)
        e1b = sbuf("e1b", [128, ND], F32)
        e2b = sbuf("e2b", [128, ND], F32)
        e1 = sbuf("e1", [128, ND], F32)
        e2 = sbuf("e2", [128, ND], F32)
        e12b = sbuf("e12b", [128, ND], F32)
        s1c = sbuf("s1c", [128, ND], F32)
        dd = sbuf("dd", [128, ND], F32)
        gt = sbuf("gt", [128, 64], F32)
        mskt = sbuf("mskt", [128, 8], F32)
        gs1 = sbuf("gs1", [128, 64], F32)
        gs2 = sbuf("gs2", [128, 64], F32)
        wsc = sbuf("wsc", [128, 1], F32)
        sbcol = sbuf("sbcol", [128, ND], F32)
        cyb = sbuf("cyb", [128, ND], F32)
        scol = sbuf("scol", [128, ND], F32)
        sg = sbuf("sg", [128, 64], F32)
        cya = sbuf("cya", [128, ND], F32)
        stat = sbuf("stat", [128, 24], F32)
        inv1 = sbuf("inv1", [128, NT], F32)
        sclq = sbuf("sclq", [128, NT], F32)
        bisq = sbuf("bisq", [128, NT], F32)
        wsoa = sbuf("wsoa", [128, NT], F32)
        inv2 = sbuf("inv2", [128, NT], F32)
        bis2 = sbuf("bis2", [128, NT], F32)
        t0c = sbuf("t0c", [128, NT], F32)
        t1c = sbuf("t1c", [128, NT], F32)
        t2c = sbuf("t2c", [128, NT], F32)

        ps_v = [psum("ps_v0", [128, TC], F32), psum("ps_v1", [128, TC], F32)]
        ps_o = [psum("ps_o0", [128, TC], F32), psum("ps_o1", [128, TC], F32)]
        ps_f = [psum("ps_f0", [128, TC], F32), psum("ps_f1", [128, TC], F32)]
        ps_t = [psum("ps_t0", [128, TC], BF16), psum("ps_t1", [128, TC], BF16)]

        semnames = ("LD", "XT0", "XT1", "GX", "GW1", "GW2", "GW3", "CI", "CC",
                    "CG", "PE", "AC", "DV", "GP", "ST0", "ST1")
        sems = {n: es.enter_context(nc.semaphore(n)) for n in semnames}

        S = Sched()

        def dma(eng, dst, src, sem_, waits=()):
            return S.emit(eng, lambda e, d=dst, s=src: e.dma_start(out=d, in_=s),
                          waits=waits, inc=(sem_, 16))

        def op(eng, fn, waits=(), sem_=None):
            return S.emit(eng, fn, waits=waits,
                          inc=(sem_, 1) if (sem_ and eng == "tensor") else None)

        dma("sync", tdq[:], d_tdt, "LD")
        dma("sync", tfq[:], d_tft, "LD")
        dma("sync", iot, d_iota, "LD")
        dma("sync", mskt[:], d_mask, "LD")
        dma("sync", gs1[:], d_gsel, "LD")
        dma("sync", gs2[:], d_gselc, "LD")
        dma("sync", wsc[:], d_wscol, "LD")
        ev_consts = dma("sync", idn[:], d_ident, "LD")

        dma("gpsimd", xfm16[:, 0:ND * TC].rearrange("p (ct t) -> p ct t", ct=ND),
            d_xfm[0].rearrange("(ct p) t -> p ct t", p=128), "GX")
        dma("gpsimd", wv16[:].rearrange("p (ct d) -> p ct d", ct=ND),
            d_wvT.rearrange("(ct p) d -> p ct d", p=128), "GW1")
        dma("gpsimd", wr16[:].rearrange("p (ct d) -> p ct d", ct=ND),
            d_wrT.rearrange("(ct p) d -> p ct d", p=128), "GW2")
        dma("gpsimd", wo16[:].rearrange("p (dt e) -> p dt e", dt=ND),
            d_woT.rearrange("(dt p) e -> p dt e", p=128), "GW3")

        wl = (("LD", ev_consts),)
        op("vector", lambda e: e.tensor_scalar(out=tdq[:], in0=tdq[:], scalar1=-10000.0,
                                               scalar2=10000.0, op0=ALU.add, op1=ALU.add), waits=wl)
        op("vector", lambda e: e.tensor_scalar(out=tfq[:], in0=tfq[:], scalar1=-10000.0,
                                               scalar2=10000.0, op0=ALU.add, op1=ALU.add))
        op("vector", lambda e: e.tensor_scalar(out=mneg[:], in0=tdq[:], scalar1=-1.0,
                                               scalar2=0.0, op0=ALU.mult, op1=ALU.min))
        op("vector", lambda e: e.tensor_scalar(out=e2b[:], in0=tdq[:], scalar1=0.0,
                                               scalar2=None, op0=ALU.min))
        op("vector", lambda e: e.tensor_scalar(out=e1[:], in0=tfq[:], scalar1=-1.0,
                                               scalar2=0.0, op0=ALU.mult, op1=ALU.min))
        ev_coef = op("vector", lambda e: e.tensor_scalar(out=e2[:], in0=tfq[:], scalar1=0.0,
                                                         scalar2=None, op0=ALU.min), sem_="DV")

        wc = (("DV", ev_coef),)
        op("scalar", lambda e: e.activation(out=e1b[:], in_=mneg[:], func=AF.Exp), waits=wc)
        op("scalar", lambda e: e.activation(out=e2b[:], in_=e2b[:], func=AF.Exp))
        op("scalar", lambda e: e.activation(out=e1[:], in_=e1[:], func=AF.Exp))
        op("scalar", lambda e: e.activation(out=e2[:], in_=e2[:], func=AF.Exp))
        ev_exps = op("scalar", lambda e: e.activation(out=dd[:], in_=mneg[:], func=AF.Exp,
                                                      scale=512.0), sem_="AC")

        we = (("AC", ev_exps),)
        op("vector", lambda e: e.tensor_mul(e12b[:], e1[:], e2b[:]), waits=we)
        op("vector", lambda e: e.tensor_scalar(out=s1c[:], in0=e2[:], scalar1=1e-6,
                                               scalar2=None, op0=ALU.add))
        gt3 = gt[:].rearrange("p (u c) -> p u c", u=8)
        dd3 = dd[:].unsqueeze(1).broadcast_to([128, 8, ND])
        gs13 = gs1[:].rearrange("p (u c) -> p u c", u=8)
        gs23 = gs2[:].rearrange("p (u c) -> p u c", u=8)
        op("vector", lambda e: e.tensor_tensor(out=gt3, in0=dd3, in1=gs13, op=ALU.mult))
        ev_post = op("vector", lambda e: e.tensor_add(gt3, gt3, gs23), sem_="DV")

        for dt in range(ND):
            op("vector", lambda e, dt=dt: e.tensor_tensor_scan(
                out=rcpd[:, dt * TC + 1:(dt + 1) * TC],
                data0=e1b[:, dt:dt + 1].broadcast_to([128, TC - 1]),
                data1=e12b[:, dt:dt + 1].broadcast_to([128, TC - 1]),
                initial=0.0, op0=ALU.mult, op1=ALU.add))
            op("vector", lambda e, dt=dt: e.scalar_tensor_tensor(
                out=sbcol[:, dt:dt + 1], in0=rcpd[:, (dt + 1) * TC - 1:(dt + 1) * TC],
                scalar=e1b[:, dt:dt + 1], in1=e12b[:, dt:dt + 1],
                op0=ALU.mult, op1=ALU.add))
        op("vector", lambda e: e.tensor_scalar(out=cyb[:], in0=sbcol[:],
                                               scalar1=mskt[:, 0:1], scalar2=None, op0=ALU.mult))
        for j in range(1, 8):
            op("vector", lambda e, j=j: e.tensor_mul(cyb[:], cyb[:], gt[:, j * 8:(j + 1) * 8]))
            op("vector", lambda e, j=j: e.scalar_tensor_tensor(
                out=cyb[:], in0=sbcol[:], scalar=mskt[:, j:j + 1], in1=cyb[:],
                op0=ALU.mult, op1=ALU.add))
        ev_cyb = op("vector", lambda e: e.engine_nop(), sem_="DV")

        ev_powt = None
        for dt in range(ND):
            ev_powt = op("scalar", lambda e, dt=dt: e.activation(
                out=powb[:, dt * TC:(dt + 1) * TC], in_=iot, func=AF.Exp,
                scale=mneg[:, dt:dt + 1]),
                waits=(("DV", ev_cyb),) if dt == 0 else ())
        ev_denshift = [None] * ND
        for dt in range(ND):
            op("vector", lambda e, dt=dt: e.scalar_tensor_tensor(
                out=rcpd[:, dt * TC + 1:(dt + 1) * TC], in0=powb[:, dt * TC + 1:(dt + 1) * TC],
                scalar=cyb[:, dt:dt + 1], in1=rcpd[:, dt * TC + 1:(dt + 1) * TC],
                op0=ALU.mult, op1=ALU.add), waits=(("AC", ev_powt),) if dt == 0 else ())
            op("vector", lambda e, dt=dt: e.tensor_copy(
                out=rcpd[:, dt * TC:dt * TC + 1], in_=cyb[:, dt:dt + 1]))
            ev_denshift[dt] = op("vector", lambda e, dt=dt: e.tensor_scalar(
                out=rcpd[:, dt * TC:(dt + 1) * TC], in0=rcpd[:, dt * TC:(dt + 1) * TC],
                scalar1=s1c[:, dt:dt + 1], scalar2=None, op0=ALU.add), sem_="DV")

        op("scalar", lambda e: e.activation(out=rcpd[:], in_=rcpd[:], func=AF.Ln),
           waits=(("DV", ev_denshift[ND - 1]),))
        ev_rcpd = op("scalar", lambda e: e.activation(out=rcpd[:], in_=rcpd[:],
                                                      func=AF.Exp, scale=-1.0), sem_="AC")

        ev_unp = [None] * ND
        ev_pk = [None] * ND
        for it in range(ND):
            w_pk = [] if it == 0 else [("DV", ev_unp[it - 1])]
            ev_pk[it] = dma("sync", sint[:, 0:256], d_pkT[128 * it:128 * (it + 1), :],
                            "LD", waits=tuple(w_pk))
            for h in range(4):
                op("vector", lambda e, h=h: e.tensor_scalar(
                    out=sint[:, 256:512], in0=sint[:, 0:256], scalar1=2 * h,
                    scalar2=3, op0=ALU.logical_shift_right, op1=ALU.bitwise_and),
                    waits=(("LD", ev_pk[it]),) if h == 0 else ())
                op("vector", lambda e: e.tensor_copy(out=sf32[:, 0:256],
                                                     in_=sint[:, 256:512]))
                ev = op("vector", lambda e, it=it, h=h: e.tensor_scalar(
                    out=wt16[:, it * C + 256 * h:it * C + 256 * (h + 1)],
                    in0=sf32[:, 0:256], scalar1=1.0, scalar2=None,
                    op0=ALU.subtract), sem_="DV" if h == 3 else None)
            ev_unp[it] = ev
        ev_bias = dma("sync", sf32[:], d_biast, "LD", waits=(("DV", ev_unp[ND - 1]),))

        ev_vmm = [[None] * ND for _ in range(B)]
        ev_evac = [[None] * ND for _ in range(B)]
        ev_p1 = [[None] * ND for _ in range(B)]
        ev_rmm = [[None] * ND for _ in range(B)]
        ev_rev = [[None] * ND for _ in range(B)]
        ev_pow = [[None] * ND for _ in range(B)]
        ev_corr = [[None] * ND for _ in range(B)]
        ev_ci = [None] * B
        ev_cg = [None] * B
        ev_cya = [None] * B
        ev_rr = [None] * B
        ev_rwkv = [None] * B
        ev_omm = [[[None] * 2 for _ in range(NT)] for _ in range(B)]
        ev_y1 = [[[None] * 2 for _ in range(NT)] for _ in range(B)]
        ev_st1 = [[None] * NT for _ in range(B)]
        ev_ch1 = [None] * B
        ev_xq = [[None] * NT for _ in range(B)]
        ev_x16 = [[None] * NT for _ in range(B)]
        ev_tp = [[None] * ND for _ in range(B)]
        ev_tev = [[None] * ND for _ in range(B)]
        ev_fmm = [[[None] * 2 for _ in range(NT)] for _ in range(B)]
        ev_y2 = [[[None] * 2 for _ in range(NT)] for _ in range(B)]
        ev_st2 = [[None] * NT for _ in range(B)]
        ev_ch2 = [None] * B
        ev_out = [[None] * NT for _ in range(B)]
        ev_st = [[None] * NT for _ in range(B)]
        ev_gx = [16, 0, 0, 0]
        xtk_last = [None]
        out_last = [None]

        def nh(b):
            return (b % 2) * ND * TC

        def xh(b):
            return (b % 2) * ND * TC

        psv_last = [None, None]

        def pe_v(b, lo=0, hi=ND, first_extra=()):
            for dt in range(lo, hi):
                _pe_v_dt(b, dt, first_extra if dt == lo else ())
                _act_evacs_dt(b, dt)

        def _pe_v_dt(b, dt, extra):
            if True:
                w = [("GX", ev_gx[b])] + list(extra)
                if b == 0 and dt == 0:
                    w.append(("GW1", 16))
                if psv_last[dt % 2] is not None:
                    w.append(("AC", psv_last[dt % 2]))
                ev = None
                for ct in range(ND):
                    ev = op("tensor", lambda e, b=b, dt=dt, ct=ct: e.matmul(
                        ps_v[dt % 2][:], wv16[:, ct * C + dt * 128:ct * C + (dt + 1) * 128],
                        xfm16[:, xh(b) + ct * TC:xh(b) + (ct + 1) * TC],
                        start=(ct == 0), stop=(ct == ND - 1)),
                        waits=tuple(w) if ct == 0 else (),
                        sem_="PE" if ct == ND - 1 else None)
                ev_vmm[b][dt] = ev

        def pe_r(b):
            for dt in range(ND):
                _pe_r_dt(b, dt)
                _act_rev_dt(b, dt)

        def _pe_r_dt(b, dt):
            if True:
                w = []
                if psv_last[dt % 2] is not None:
                    w.append(("AC", psv_last[dt % 2]))
                if b == 0 and dt == 0:
                    w.append(("GW2", 16))
                ev = None
                for ct in range(ND):
                    ev = op("tensor", lambda e, b=b, dt=dt, ct=ct: e.matmul(
                        ps_v[dt % 2][:], wr16[:, ct * C + dt * 128:ct * C + (dt + 1) * 128],
                        xfm16[:, xh(b) + ct * TC:xh(b) + (ct + 1) * TC],
                        start=(ct == 0), stop=(ct == ND - 1)),
                        waits=tuple(w) if ct == 0 else (),
                        sem_="PE" if ct == ND - 1 else None)
                ev_rmm[b][dt] = ev

        def pe_out(b):
            for tc in range(NT):
                for eh in range(2):
                    w = [("DV", ev_rwkv[b])]
                    if b == 0 and tc == 0 and eh == 0:
                        w.append(("GW3", 16))
                    if tc > 0:
                        w.append(("AC", lambda b=b, tc=tc, eh=eh: ev_y1[b][tc - 1][eh]))
                    elif b > 0:
                        w.append(("AC", ev_y1[b - 1][NT - 1][eh]))
                    ev = None
                    for dt in range(ND):
                        ev = op("tensor", lambda e, dt=dt, tc=tc, eh=eh: e.matmul(
                            ps_o[eh][:], rw16[:, dt * TC + tc * 128:dt * TC + (tc + 1) * 128],
                            wo16[:, dt * C + eh * TC:dt * C + (eh + 1) * TC],
                            start=(dt == 0), stop=(dt == ND - 1)),
                            waits=tuple(w) if dt == 0 else (),
                            sem_="PE" if dt == ND - 1 else None)
                    ev_omm[b][tc][eh] = ev

        def pe_tp(b):
            for it in range(ND):
                _pe_tp_it(b, it)
                _act_tev_it(b, it)

        def _pe_tp_it(b, it):
            if True:
                w_tp = [("DV", ev_x16[b][NT - 1])] if it == 0 else []
                if b == 0 and it == 0:
                    w_tp.append(("LD", ev_consts))
                if it >= 2:
                    w_tp.append(("AC", ev_tev[b][it - 2]))
                elif b > 0:
                    w_tp.append(("AC", ev_tev[b - 1][it + 6]))
                ev = None
                for tc in range(NT):
                    ev = op("tensor", lambda e, it=it, tc=tc: e.transpose(
                        ps_t[it % 2][:, tc * 128:(tc + 1) * 128],
                        rw16[:, tc * C + it * 128:tc * C + (it + 1) * 128], idn[:]),
                        waits=tuple(w_tp) if tc == 0 else (),
                        sem_="PE" if tc == NT - 1 else None)
                ev_tp[b][it] = ev

        def pe_ffn(b):
            for tc in range(NT):
                for oh in range(2):
                    w = [("AC", ev_tev[b][ND - 1])]
                    if b == 0 and tc == 0 and oh == 0:
                        w.append(("DV", ev_unp[ND - 1]))
                        w.append(("LD", ev_bias))
                    if tc > 0:
                        w.append(("DV", lambda b=b, tc=tc, oh=oh: ev_y2[b][tc - 1][oh]))
                    elif b > 0:
                        w.append(("DV", ev_y2[b - 1][NT - 1][oh]))
                    ev = None
                    for it in range(ND):
                        ev = op("tensor", lambda e, it=it, tc=tc, oh=oh: e.matmul(
                            ps_f[oh][:], xqT16[:, it * TC + tc * 128:it * TC + (tc + 1) * 128],
                            wt16[:, it * C + oh * TC:it * C + (oh + 1) * TC],
                            start=(it == 0), stop=(it == ND - 1)),
                            waits=tuple(w) if it == 0 else (),
                            sem_="PE" if it == ND - 1 else None)
                    ev_fmm[b][tc][oh] = ev

        def _act_evacs_dt(b, dt):
            if True:
                w_ev = [("PE", ev_vmm[b][dt])]
                if dt == 0 and b >= 2:
                    w_ev.append(("DV", ev_rwkv[b - 2]))
                if dt < 2 and b > 0:
                    w_ev.append(("DV", ev_p1[b - 1][dt + 6]))
                op("scalar", lambda e, b=b, dt=dt: e.activation(
                    out=num[:, nh(b) + dt * TC:nh(b) + (dt + 1) * TC],
                    in_=ps_v[dt % 2][:], func=AF.Copy,
                    scale=e2[:, dt:dt + 1]), waits=tuple(w_ev))
                w2 = []
                if dt >= 2:
                    w2.append(("DV", lambda b=b, dt=dt: ev_p1[b][dt - 2]))
                ev_evac[b][dt] = op("scalar", lambda e, dt=dt: e.activation(
                    out=scn[:, (dt % 2) * TC:(dt % 2 + 1) * TC], in_=ps_v[dt % 2][:],
                    func=AF.Copy, scale=e12b[:, dt:dt + 1]), waits=tuple(w2))
                psv_last[dt % 2] = ev_evac[b][dt]

        def _act_rev_dt(b, dt):
            if True:
                w = [("PE", ev_rmm[b][dt])]
                if dt == 0 and b > 0:
                    w.append(("PE", ev_tp[b - 1][ND - 1]))
                ev_rev[b][dt] = op("scalar", lambda e, b=b, dt=dt: e.activation(
                    out=rw16[:, dt * TC:(dt + 1) * TC], in_=ps_v[dt % 2][:],
                    func=AF.Tanh, scale=0.5), waits=tuple(w))
                psv_last[dt % 2] = ev_rev[b][dt]

        def act_st1(b):
            for tc in range(NT):
                op("scalar", lambda e, tc=tc: e.activation(
                    out=xqi8[:].bitcast(F32), in_=y1b[:, tc * C:(tc + 1) * C], func=AF.Copy,
                    accum_out=stat[:, tc:tc + 1]), waits=(("XT0", ev_acc[b][tc]),))
                ev_st1[b][tc] = op("scalar", lambda e, tc=tc: e.activation(
                    out=xqi8[:].bitcast(F32), in_=y1b[:, tc * C:(tc + 1) * C], func=AF.Square,
                    accum_out=stat[:, 4 + tc:5 + tc]))

        def act_u(b):
            for tc in range(NT):
                w_xq = [("DV", ev_ch1[b])]
                if tc > 0:
                    w_xq.append(("DV", lambda b=b, tc=tc: ev_xq[b][tc - 1]))
                ev = op("scalar", lambda e, tc=tc: e.activation(
                    out=denb[:], in_=y1b[:, tc * C:(tc + 1) * C],
                    func=AF.Identity, scale=sclq[:, tc:tc + 1], bias=bisq[:, tc:tc + 1]),
                    waits=tuple(w_xq))
                ev_u[b][tc] = ev

        def _act_tev_it(b, it):
            if True:
                w_tev = [("PE", ev_tp[b][it])]
                if b > 0 and it == 0:
                    w_tev.append(("PE", ev_fmm[b - 1][NT - 1][1]))
                ev_tev[b][it] = op("scalar", lambda e, it=it: e.activation(
                    out=xqT16[:, it * TC:(it + 1) * TC], in_=ps_t[it % 2][:],
                    func=AF.Copy), waits=tuple(w_tev))

        def act_st2(b):
            for tc in range(NT):
                op("scalar", lambda e, tc=tc: e.activation(
                    out=xqi8[:].bitcast(F32), in_=y1b[:, tc * C:(tc + 1) * C], func=AF.Copy,
                    accum_out=stat[:, 16 + tc:17 + tc]), waits=(("DV", ev_y2[b][tc][1]),))
                ev_st2[b][tc] = op("scalar", lambda e, tc=tc: e.activation(
                    out=xqi8[:].bitcast(F32), in_=y1b[:, tc * C:(tc + 1) * C], func=AF.Square,
                    accum_out=stat[:, 20 + tc:21 + tc]))

        def act_out(b):
            for tc in range(NT):
                w_o = [("DV", ev_ch2[b])]
                if out_last[0] is not None:
                    w_o.append(("ST0", out_last[0]))
                ev_out[b][tc] = op("scalar", lambda e, tc=tc: e.activation(
                    out=outb[:], in_=y1b[:, tc * C:(tc + 1) * C], func=AF.Identity,
                    scale=inv2[:, tc:tc + 1], bias=bis2[:, tc:tc + 1]),
                    waits=tuple(w_o))
                ev_st[b][tc] = dma("sync", d_y[b, tc * 128:(tc + 1) * 128, :],
                                   outb[:], "ST0", waits=(("AC", ev_out[b][tc]),))
                out_last[0] = ev_st[b][tc]

        def v_p1(b):
            for dt in range(ND):
                w_sc = [("AC", ev_evac[b][dt])]
                if b == 0 and dt == 0:
                    w_sc.append(("DV", ev_post))
                if dt == 0:
                    w_sc.append(("CI", 16 * b))
                h = (dt % 2) * TC
                op("vector", lambda e, dt=dt, h=h: e.tensor_tensor_scan(
                    out=scn[:, h:h + TC],
                    data0=e1b[:, dt:dt + 1].broadcast_to([128, TC]),
                    data1=scn[:, h:h + TC],
                    initial=0.0, op0=ALU.mult, op1=ALU.add), waits=tuple(w_sc))
                op("vector", lambda e, b=b, dt=dt, h=h: e.tensor_add(
                    num[:, nh(b) + dt * TC + 1:nh(b) + (dt + 1) * TC],
                    scn[:, h:h + TC - 1],
                    num[:, nh(b) + dt * TC + 1:nh(b) + (dt + 1) * TC]))
                ev_p1[b][dt] = op("vector", lambda e, dt=dt, h=h: e.tensor_copy(
                    out=scol[:, dt:dt + 1], in_=scn[:, h + TC - 1:h + TC]))

        def v_carry(b):
            op("vector", lambda e: e.tensor_scalar(
                out=cya[:], in0=sg[:, 0:8], scalar1=mskt[:, 0:1], scalar2=None,
                op0=ALU.mult), waits=(("CG", ev_cg[b]),))
            for j in range(1, 8):
                op("vector", lambda e, j=j: e.tensor_mul(cya[:], cya[:], gt[:, j * 8:(j + 1) * 8]))
                op("vector", lambda e, j=j: e.scalar_tensor_tensor(
                    out=cya[:], in0=sg[:, j * 8:(j + 1) * 8], scalar=mskt[:, j:j + 1],
                    in1=cya[:], op0=ALU.mult, op1=ALU.add))
            ev_cya[b] = op("vector", lambda e: e.engine_nop())

        def v_corrwkv(b):
            for dt in range(ND):
                ev_corr[b][dt] = op("vector", lambda e, b=b, dt=dt: e.scalar_tensor_tensor(
                    out=num[:, nh(b) + dt * TC:nh(b) + (dt + 1) * TC],
                    in0=powb[:, dt * TC:(dt + 1) * TC],
                    scalar=cya[:, dt:dt + 1],
                    in1=num[:, nh(b) + dt * TC:nh(b) + (dt + 1) * TC],
                    op0=ALU.mult, op1=ALU.add))

        def g_wkv(b):
            w_wkv = [("DV", ev_corr[b][ND - 1])]
            if b == 0:
                w_wkv.append(("AC", ev_rcpd))
            ev_wkv[b] = S.emit("gpsimd", lambda e, b=b: e.tensor_mul(
                num[:, nh(b):nh(b) + ND * TC], num[:, nh(b):nh(b) + ND * TC], rcpd[:]),
                waits=tuple(w_wkv), inc=("GP", 1))

        def v_rwkv(b):
            for dt in range(ND):
                ev = op("vector", lambda e, b=b, dt=dt: e.tensor_mul(
                    rw16[:, dt * TC:(dt + 1) * TC],
                    num[:, nh(b) + dt * TC:nh(b) + (dt + 1) * TC],
                    rw16[:, dt * TC:(dt + 1) * TC]),
                    waits=(("GP", lambda b=b: ev_wkv[b]),) if dt == 0 else ())
            ev_rwkv[b] = ev

        ev_wkv = [None] * B
        ev_acc = [[None] * NT for _ in range(B)]
        ev_mxmn = [None] * B

        def v_y1(b):
            for tc in range(NT):
                for eh in range(2):
                    w_y1 = [("PE", ev_omm[b][tc][eh])]
                    if b > 0 and eh == 0:
                        w_y1.append(("AC", ev_out[b - 1][tc]))
                    ev_y1[b][tc][eh] = op("scalar", lambda e, b=b, tc=tc, eh=eh: e.activation(
                        out=y1b[:, tc * C + eh * TC:tc * C + (eh + 1) * TC],
                        in_=ps_o[eh][:], func=AF.Copy), waits=tuple(w_y1))
                ev_acc[b][tc] = S.emit(
                    "gpsimd",
                    lambda e, b=b, tc=tc: e.dma_start(
                        out=y1b[:, tc * C:(tc + 1) * C],
                        in_=d_xtok[b, tc * 128:(tc + 1) * 128, :],
                        accum_op=ALU.add),
                    waits=(("AC", ev_y1[b][tc][1]),), inc=("XT0", 16))
                op("vector", lambda e, tc=tc: e.tensor_reduce(
                    out=stat[:, 8 + tc:9 + tc], in_=y1b[:, tc * C:(tc + 1) * C],
                    axis=AX.X, op=ALU.max), waits=(("XT0", ev_acc[b][tc]),))
                ev = op("vector", lambda e, tc=tc: e.tensor_reduce(
                    out=stat[:, 12 + tc:13 + tc], in_=y1b[:, tc * C:(tc + 1) * C],
                    axis=AX.X, op=ALU.min))
            ev_mxmn[b] = ev

        def v_chain1(b):
            op("vector", lambda e: e.tensor_scalar(
                out=t0c[:], in0=stat[:, 0:4], scalar1=1.0 / C, scalar2=None,
                op0=ALU.mult), waits=(("AC", ev_st1[b][NT - 1]), ("DV", ev_mxmn[b])))
            op("vector", lambda e: e.tensor_mul(t1c[:], t0c[:], t0c[:]))
            op("vector", lambda e: e.scalar_tensor_tensor(
                out=t1c[:], in0=stat[:, 4:8], scalar=1.0 / C, in1=t1c[:],
                op0=ALU.mult, op1=ALU.subtract))
            op("vector", lambda e: e.tensor_scalar(
                out=t1c[:], in0=t1c[:], scalar1=1e-5, scalar2=None, op0=ALU.add))
            op("vector", lambda e: e.tensor_scalar(
                out=inv1[:].bitcast(I32), in0=t1c[:].bitcast(I32), scalar1=1,
                scalar2=None, op0=ALU.logical_shift_right))
            op("vector", lambda e: e.tensor_scalar(
                out=inv1[:].bitcast(I32), in0=inv1[:].bitcast(I32), scalar1=-1,
                scalar2=RSQRT_MAGIC, op0=ALU.mult, op1=ALU.add))
            for _ in range(2):
                op("vector", lambda e: e.tensor_mul(t2c[:], t1c[:], inv1[:]))
                op("vector", lambda e: e.tensor_mul(t2c[:], t2c[:], inv1[:]))
                op("vector", lambda e: e.tensor_scalar(
                    out=t2c[:], in0=t2c[:], scalar1=-0.5, scalar2=1.5,
                    op0=ALU.mult, op1=ALU.add))
                op("vector", lambda e: e.tensor_mul(inv1[:], inv1[:], t2c[:]))
            op("vector", lambda e: e.tensor_sub(t2c[:], stat[:, 8:12], t0c[:]))
            op("vector", lambda e: e.tensor_sub(t1c[:], t0c[:], stat[:, 12:16]))
            op("vector", lambda e: e.tensor_max(t2c[:], t2c[:], t1c[:]))
            op("vector", lambda e: e.tensor_mul(t2c[:], t2c[:], inv1[:]))
            op("vector", lambda e: e.tensor_scalar(
                out=t2c[:], in0=t2c[:], scalar1=1e-5, scalar2=None, op0=ALU.max))
            op("vector", lambda e: e.tensor_scalar(
                out=wsoa[:], in0=t2c[:], scalar1=wsc[:], scalar2=1.0 / 127.0,
                op0=ALU.mult, op1=ALU.mult))
            op("vector", lambda e: e.reciprocal(out=sclq[:], in_=t2c[:]))
            op("vector", lambda e: e.tensor_scalar(
                out=sclq[:], in0=sclq[:], scalar1=127.0, scalar2=None, op0=ALU.mult))
            op("vector", lambda e: e.tensor_mul(sclq[:], sclq[:], inv1[:]))
            op("vector", lambda e: e.tensor_mul(bisq[:], t0c[:], sclq[:]))
            ev_ch1[b] = op("vector", lambda e: e.tensor_scalar(
                out=bisq[:], in0=bisq[:], scalar1=-1.0, scalar2=None, op0=ALU.mult))

        def v_quant(b):
            for tc in range(NT):
                ev_xq[b][tc] = op("vector", lambda e, tc=tc: e.tensor_scalar(
                    out=xqi8[:, tc * C:(tc + 1) * C], in0=denb[:], scalar1=12582912.0,
                    scalar2=-12582912.0, op0=ALU.add, op1=ALU.add),
                    waits=(("AC", ev_u[b][tc]),))

        def v_y2(b):
            for tc in range(NT):
                for oh in range(2):
                    op("vector", lambda e, b=b, tc=tc, oh=oh: e.scalar_tensor_tensor(
                        out=denb[:, oh * TC:(oh + 1) * TC], in0=ps_f[oh][:],
                        scalar=wsoa[:, tc:tc + 1], in1=sf32[:, oh * TC:(oh + 1) * TC],
                        op0=ALU.mult, op1=ALU.add),
                        waits=(("PE", ev_fmm[b][tc][oh]), ("DV", ev_xq[b][NT - 1])))
                    ev_y2[b][tc][oh] = op("vector", lambda e, tc=tc, oh=oh: e.scalar_tensor_tensor(
                        out=y1b[:, tc * C + oh * TC:tc * C + (oh + 1) * TC],
                        in0=y1b[:, tc * C + oh * TC:tc * C + (oh + 1) * TC],
                        scalar=inv1[:, tc:tc + 1],
                        in1=denb[:, oh * TC:(oh + 1) * TC],
                        op0=ALU.mult, op1=ALU.add))

        def v_chain2(b):
            op("vector", lambda e: e.tensor_scalar(
                out=t0c[:], in0=stat[:, 16:20], scalar1=1.0 / C, scalar2=None,
                op0=ALU.mult), waits=(("AC", ev_st2[b][NT - 1]),))
            op("vector", lambda e: e.tensor_mul(t1c[:], t0c[:], t0c[:]))
            op("vector", lambda e: e.scalar_tensor_tensor(
                out=t1c[:], in0=stat[:, 20:24], scalar=1.0 / C, in1=t1c[:],
                op0=ALU.mult, op1=ALU.subtract))
            op("vector", lambda e: e.tensor_scalar(
                out=t1c[:], in0=t1c[:], scalar1=1e-5, scalar2=None, op0=ALU.add))
            op("vector", lambda e: e.tensor_scalar(
                out=inv2[:].bitcast(I32), in0=t1c[:].bitcast(I32), scalar1=1,
                scalar2=None, op0=ALU.logical_shift_right))
            op("vector", lambda e: e.tensor_scalar(
                out=inv2[:].bitcast(I32), in0=inv2[:].bitcast(I32), scalar1=-1,
                scalar2=RSQRT_MAGIC, op0=ALU.mult, op1=ALU.add))
            for _ in range(3):
                op("vector", lambda e: e.tensor_mul(t2c[:], t1c[:], inv2[:]))
                op("vector", lambda e: e.tensor_mul(t2c[:], t2c[:], inv2[:]))
                op("vector", lambda e: e.tensor_scalar(
                    out=t2c[:], in0=t2c[:], scalar1=-0.5, scalar2=1.5,
                    op0=ALU.mult, op1=ALU.add))
                op("vector", lambda e: e.tensor_mul(inv2[:], inv2[:], t2c[:]))
            op("vector", lambda e: e.tensor_mul(bis2[:], t0c[:], inv2[:]))
            ev_ch2[b] = op("vector", lambda e: e.tensor_scalar(
                out=bis2[:], in0=bis2[:], scalar1=-1.0, scalar2=None, op0=ALU.mult))

        def g_gather(b):
            ev_ci[b] = dma("sync", d_ccin[b].rearrange("(p f) -> p f", p=128), scol[:],
                           "CI", waits=(("DV", ev_p1[b][ND - 1]),))
            S.emit("gpsimd", lambda e, b=b: e.collective_compute(
                "AllGather", ALU.bypass, ins=[d_ccin[b]], outs=[d_ccout[b]],
                replica_groups=[[0, 1, 2, 3, 4, 5, 6, 7]]),
                waits=(("CI", ev_ci[b]),), inc=("CC", 1))
            w_sg = [("CC", b + 1)]
            if b > 0:
                w_sg.append(("DV", ev_cya[b - 1]))
            ev_cg[b] = dma("gpsimd", sg[:].rearrange("p (j f) -> p j f", j=8),
                           d_ccout[b].rearrange("j (p f) -> p j f", p=128), "CG",
                           waits=tuple(w_sg))

        def g_rr(b):
            for dt in range(ND):
                ev_rr[b] = S.emit("gpsimd", lambda e, dt=dt: e.tensor_scalar(
                    out=rw16[:, dt * TC:(dt + 1) * TC], in0=rw16[:, dt * TC:(dt + 1) * TC],
                    scalar1=0.5, scalar2=0.5, op0=ALU.mult, op1=ALU.add),
                    waits=(("AC", ev_rev[b][dt]),))

        def v_x16(b):
            for tc in range(NT):
                w = []
                if tc == 0:
                    w.append(("PE", ev_omm[b][NT - 1][1]))
                ev_x16[b][tc] = op("vector", lambda e, tc=tc: e.tensor_copy(
                    out=rw16[:, tc * C:(tc + 1) * C], in_=xqi8[:, tc * C:(tc + 1) * C]),
                    waits=tuple(w))

        def g_xfm(b):
            ev_gx[b] = dma("gpsimd", xfm16[:, xh(b):xh(b) + ND * TC].rearrange(
                "p (ct t) -> p ct t", ct=ND),
                d_xfm[b].rearrange("(ct p) t -> p ct t", p=128), "GX",
                waits=(("PE", ev_rmm[b - 2][ND - 1]),))

        ev_u = [[None] * NT for _ in range(B)]

        pe_v(0)
        v_p1(0)
        g_gather(0)
        pe_r(0)
        g_rr(0)
        ev_gx[1] = dma("gpsimd", xfm16[:, ND * TC:2 * ND * TC].rearrange(
            "p (ct t) -> p ct t", ct=ND),
            d_xfm[1].rearrange("(ct p) t -> p ct t", p=128), "GX")
        pe_v(1)
        v_carry(0)

        v_corrwkv(0)
        g_wkv(0)
        for b in range(B):
            v_rwkv(b)
            if b > 0:
                for kv in range(ev_y2[b - 1][0][0], ev_rwkv[b], 4):
                    op("tensor", lambda e: e.matmul(
                        ps_o[0][:, 0:8], idn[:], idn[:, 0:8],
                        start=True, stop=True), waits=(("DV", kv),))
            pe_out(b)
            if b + 2 < B:
                g_xfm(b + 2)
            if b + 1 < B:
                v_p1(b + 1)
            v_y1(b)
            if b + 2 < B:
                pe_v(b + 2, 0, 2)
            act_st1(b)
            v_chain1(b)
            act_u(b)
            v_quant(b)
            v_x16(b)
            if b + 1 < B:
                g_gather(b + 1)
                v_carry(b + 1)
                v_corrwkv(b + 1)
            w0 = [("AC", ev_y1[b][NT - 1][0])]
            if b == 0:
                w0.append(("LD", ev_consts))
            for kv in range(ev_mxmn[b] + 2, ev_x16[b][NT - 1], 3):
                op("tensor", lambda e: e.matmul(
                    ps_o[0][:, 0:8], idn[:], idn[:, 0:8],
                    start=True, stop=True), waits=tuple([("DV", kv)] + w0))
                w0 = []
            pe_tp(b)
            if b + 1 < B:
                pe_r(b + 1)
                g_rr(b + 1)
                g_wkv(b + 1)
            pe_ffn(b)
            v_y2(b)
            act_st2(b)
            v_chain2(b)
            act_out(b)
            if b + 2 < B:
                pe_v(b + 2, 2, ND)

        S.emit("sync", None, waits=(("ST0", S.cnt["ST0"]),))

        S.check()

        with nc.Block() as block:
            def runner(eng_name):
                def body(eng):
                    for fn, waits, inc in S.ops[eng_name]:
                        for s, v in waits:
                            eng.wait_ge(sems[s], v() if callable(v) else v)
                        if fn is not None:
                            instr = fn(eng)
                            if inc is not None:
                                instr.then_inc(sems[inc[0]], inc[1])
                return body
            block.sync(runner("sync"))
            block.scalar(runner("scalar"))
            block.vector(runner("vector"))
            block.tensor(runner("tensor"))
            block.gpsimd(runner("gpsimd"))

    return nc


_NC_CACHE = None


def kernel(L0_attn_x, L0_attn_time_decay, L0_attn_time_first, L0_attn_key_w,
           L0_attn_value_w, L0_attn_recep_w, L0_attn_out_w, L0_ffn_packed_weight,
           L0_ffn_weight_scale, L0_ffn_bias):
    global _NC_CACHE
    x = np.asarray(L0_attn_x, dtype=np.float32)
    td = np.asarray(L0_attn_time_decay, dtype=np.float32)
    tf = np.asarray(L0_attn_time_first, dtype=np.float32)
    wv = np.asarray(L0_attn_value_w, dtype=np.float32)
    wr = np.asarray(L0_attn_recep_w, dtype=np.float32)
    wo = np.asarray(L0_attn_out_w, dtype=np.float32)
    pk = np.asarray(L0_ffn_packed_weight, dtype=np.int32)
    ws = np.asarray(L0_ffn_weight_scale, dtype=np.float32)
    bi = np.asarray(L0_ffn_bias, dtype=np.float32)

    wvT = np.ascontiguousarray(wv.T)
    wrT = np.ascontiguousarray(wr.T)
    woT = np.ascontiguousarray(wo.T)
    pkT = np.ascontiguousarray(pk.reshape(256, C).T)
    tdt = np.ascontiguousarray(td.reshape(ND, 128).T)
    tft = np.ascontiguousarray(tf.reshape(ND, 128).T)
    iota = np.ascontiguousarray(np.broadcast_to(
        np.arange(TC, dtype=np.float32), (128, TC)))
    wscol = np.ascontiguousarray(np.broadcast_to(ws.reshape(1, 1), (128, 1)))
    ident = np.eye(128, dtype=ml_dtypes.bfloat16)
    biast = np.ascontiguousarray(np.broadcast_to(bi.reshape(1, C), (128, C)))

    in_maps = []
    for k in range(NCORES):
        sl = x[:, TC * k:TC * (k + 1), :]
        maskt = np.ascontiguousarray(np.broadcast_to(
            (np.arange(8) < k).astype(np.float32), (128, 8)))
        gsel = np.ascontiguousarray(np.broadcast_to(
            np.repeat((np.arange(8) < k).astype(np.float32), 8), (128, 64)))
        in_maps.append(dict(
            xfm=np.ascontiguousarray(sl.transpose(0, 2, 1)),
            xtok=np.ascontiguousarray(sl),
            wvT=wvT, wrT=wrT, woT=woT, pkT=pkT, tdt=tdt, tft=tft, iota=iota,
            maskt=maskt, gsel=gsel, gselc=np.ascontiguousarray(1.0 - gsel),
            wscol=wscol, ident=ident, biast=biast,
        ))

    if _NC_CACHE is None:
        _NC_CACHE = build_nc()
    res = run_bass_kernel_spmd(_NC_CACHE, in_maps, core_ids=list(range(NCORES)))
    out = np.empty((B, NCORES * TC, C), dtype=np.float32)
    for k in range(NCORES):
        out[:, TC * k:TC * (k + 1), :] = res.results[k]["y"]
    return out


# version 19 (final)
# speedup vs baseline: 1.2513x; 1.2513x over previous
import contextlib
import numpy as np
import ml_dtypes
import concourse.bass as bass
import concourse.mybir as mybir
from concourse.bass_utils import run_bass_kernel_spmd

F32 = mybir.dt.float32
BF16 = mybir.dt.bfloat16
I8 = mybir.dt.int8
I32 = mybir.dt.int32
AF = mybir.ActivationFunctionType
ALU = mybir.AluOpType
AX = mybir.AxisListType

NCORES = 8
B = 4
TC = 512
NT = 4
ND = 8
C = 1024
DEBUG_DUMPS = False
RSQRT_MAGIC = 0x5F3759DF
RCP_MAGIC = 0x7EF311C3


class Sched:

    def __init__(self):
        self.ops = {k: [] for k in ("sync", "scalar", "vector", "tensor", "gpsimd")}
        self.cnt = {}

    AUTO = {"vector": "DV", "scalar": "AC", "gpsimd": "GP"}

    def emit(self, eng, fn, waits=(), inc=None):
        waits = list(waits)
        if fn is not None and eng in self.AUTO and inc is None:
            seq = self.AUTO[eng]
            prev = self.cnt.get(seq, 0)
            if prev > 0:
                waits.append((seq, prev))
            inc = (seq, 1)
        if inc is not None:
            self.cnt[inc[0]] = self.cnt.get(inc[0], 0) + inc[1]
        self.ops[eng].append((fn, tuple(waits), inc))
        return self.cnt.get(inc[0]) if inc is not None else None

    def check(self):
        pos = {k: 0 for k in self.ops}
        sem = {}
        progress = True
        while progress:
            progress = False
            for eng, lst in self.ops.items():
                while pos[eng] < len(lst):
                    fn, waits, inc = lst[pos[eng]]
                    if all(sem.get(s, 0) >= (v() if callable(v) else v)
                           for s, v in waits):
                        if inc is not None:
                            sem[inc[0]] = sem.get(inc[0], 0) + inc[1]
                        pos[eng] += 1
                        progress = True
                    else:
                        break
        stuck = {e: p for e, p in pos.items() if p < len(self.ops[e])}
        if stuck:
            msgs = []
            for e, p in stuck.items():
                _, waits, _ = self.ops[e][p]
                unmet = [(s, (v() if callable(v) else v), sem.get(s, 0)) for s, v in waits
                         if sem.get(s, 0) < (v() if callable(v) else v)]
                msgs.append(f"{e} op {p}/{len(self.ops[e])} waits {unmet}")
            raise RuntimeError("schedule deadlock: " + "; ".join(msgs))


def build_nc():
    nc = bass.Bass("TRN2", target_bir_lowering=False,
                   detect_race_conditions=False)

    d_xfm = nc.dram_tensor("xfm", [B, C, TC], F32, kind="ExternalInput").ap()
    d_xtok = nc.dram_tensor("xtok", [B, TC, C], F32, kind="ExternalInput").ap()
    d_wvT = nc.dram_tensor("wvT", [C, C], F32, kind="ExternalInput").ap()
    d_wrT = nc.dram_tensor("wrT", [C, C], F32, kind="ExternalInput").ap()
    d_woT = nc.dram_tensor("woT", [C, C], F32, kind="ExternalInput").ap()
    d_pkT = nc.dram_tensor("pkT", [1024, 256], I32, kind="ExternalInput").ap()
    d_tdt = nc.dram_tensor("tdt", [128, ND], F32, kind="ExternalInput").ap()
    d_tft = nc.dram_tensor("tft", [128, ND], F32, kind="ExternalInput").ap()
    d_iota = nc.dram_tensor("iota", [128, TC], F32, kind="ExternalInput").ap()
    d_mask = nc.dram_tensor("maskt", [128, 8], F32, kind="ExternalInput").ap()
    d_gsel = nc.dram_tensor("gsel", [128, 64], F32, kind="ExternalInput").ap()
    d_gselc = nc.dram_tensor("gselc", [128, 64], F32, kind="ExternalInput").ap()
    d_wscol = nc.dram_tensor("wscol", [128, 1], F32, kind="ExternalInput").ap()
    d_ident = nc.dram_tensor("ident", [128, 128], BF16, kind="ExternalInput").ap()
    d_biast = nc.dram_tensor("biast", [128, C], F32, kind="ExternalInput").ap()
    d_y = nc.dram_tensor("y", [B, TC, C], F32, kind="ExternalOutput").ap()
    d_ccin = nc.dram_tensor("ccin", [B, C], F32, kind="Internal").ap()
    d_ccout = nc.dram_tensor("ccout", [B, 8, C], F32, kind="Internal").ap()

    es = contextlib.ExitStack()

    def sbuf(name, shape, dt):
        return es.enter_context(nc.sbuf_tensor(name, shape, dt))

    def psum(name, shape, dt):
        return es.enter_context(nc.psum_tensor(name, shape, dt))

    with es:
        wv16 = sbuf("wv16", [128, ND * C], BF16)
        wr16 = sbuf("wr16", [128, ND * C], BF16)
        wo16 = sbuf("wo16", [128, ND * C], BF16)
        wt16 = sbuf("wt16", [128, ND * C], BF16)
        rcpd = sbuf("rcpd", [128, ND * TC], F32)
        denb = sbuf("denb", [128, C], F32)
        powb = sbuf("powb", [128, ND * TC], F32)
        sf32 = sbuf("sf32", [128, C], F32)
        xqi8 = sbuf("xqi8", [128, NT * C], I8)
        idn = sbuf("idn", [128, 128], BF16)
        xfm16 = sbuf("xfm16", [128, 2 * ND * TC], BF16)
        xqT16 = sbuf("xqT16", [128, ND * TC], BF16)
        scn = sbuf("scn", [128, 2 * TC], F32)
        num = sbuf("num", [128, 2 * ND * TC], F32)
        rw16 = sbuf("rw16", [128, ND * TC], BF16)
        y1b = sbuf("y1b", [128, NT * C], F32)
        sint = y1b[:, 0:TC].bitcast(I32)
        outb = sbuf("outb", [128, C], F32)
        iot = outb[:, 0:TC]
        tdq = sbuf("tdq", [128, ND], F32)
        tfq = sbuf("tfq", [128, ND], F32)
        mneg = sbuf("mneg", [128, ND], F32)
        e1b = sbuf("e1b", [128, ND], F32)
        e2b = sbuf("e2b", [128, ND], F32)
        e1 = sbuf("e1", [128, ND], F32)
        e2 = sbuf("e2", [128, ND], F32)
        e12b = sbuf("e12b", [128, ND], F32)
        s1c = sbuf("s1c", [128, ND], F32)
        dd = sbuf("dd", [128, ND], F32)
        gt = sbuf("gt", [128, 64], F32)
        mskt = sbuf("mskt", [128, 8], F32)
        gs1 = sbuf("gs1", [128, 64], F32)
        gs2 = sbuf("gs2", [128, 64], F32)
        wsc = sbuf("wsc", [128, 1], F32)
        sbcol = sbuf("sbcol", [128, ND], F32)
        cyb = sbuf("cyb", [128, ND], F32)
        scol = sbuf("scol", [128, ND], F32)
        sg = sbuf("sg", [128, 64], F32)
        cya = sbuf("cya", [128, ND], F32)
        stat = sbuf("stat", [128, 24], F32)
        inv1 = sbuf("inv1", [128, NT], F32)
        sclq = sbuf("sclq", [128, NT], F32)
        bisq = sbuf("bisq", [128, NT], F32)
        wsoa = sbuf("wsoa", [128, NT], F32)
        inv2 = sbuf("inv2", [128, NT], F32)
        bis2 = sbuf("bis2", [128, NT], F32)
        t0c = sbuf("t0c", [128, NT], F32)
        t1c = sbuf("t1c", [128, NT], F32)
        t2c = sbuf("t2c", [128, NT], F32)

        ps_v = [psum("ps_v0", [128, TC], F32), psum("ps_v1", [128, TC], F32)]
        ps_o = [psum("ps_o0", [128, TC], F32), psum("ps_o1", [128, TC], F32)]
        ps_f = [psum("ps_f0", [128, TC], F32), psum("ps_f1", [128, TC], F32)]
        ps_t = [psum("ps_t0", [128, TC], BF16), psum("ps_t1", [128, TC], BF16)]

        semnames = ("LD", "XT0", "XT1", "GX", "GW1", "GW2", "GW3", "CI", "CC",
                    "CG", "PE", "AC", "DV", "GP", "ST0", "ST1")
        sems = {n: es.enter_context(nc.semaphore(n)) for n in semnames}

        S = Sched()

        def dma(eng, dst, src, sem_, waits=()):
            return S.emit(eng, lambda e, d=dst, s=src: e.dma_start(out=d, in_=s),
                          waits=waits, inc=(sem_, 16))

        def op(eng, fn, waits=(), sem_=None):
            return S.emit(eng, fn, waits=waits,
                          inc=(sem_, 1) if (sem_ and eng == "tensor") else None)

        dma("sync", tdq[:], d_tdt, "LD")
        dma("sync", tfq[:], d_tft, "LD")
        dma("sync", iot, d_iota, "LD")
        dma("sync", mskt[:], d_mask, "LD")
        dma("sync", gs1[:], d_gsel, "LD")
        dma("sync", gs2[:], d_gselc, "LD")
        dma("sync", wsc[:], d_wscol, "LD")
        ev_consts = dma("sync", idn[:], d_ident, "LD")

        dma("gpsimd", xfm16[:, 0:ND * TC].rearrange("p (ct t) -> p ct t", ct=ND),
            d_xfm[0].rearrange("(ct p) t -> p ct t", p=128), "GX")
        dma("gpsimd", wv16[:].rearrange("p (ct d) -> p ct d", ct=ND),
            d_wvT.rearrange("(ct p) d -> p ct d", p=128), "GW1")
        dma("gpsimd", wr16[:].rearrange("p (ct d) -> p ct d", ct=ND),
            d_wrT.rearrange("(ct p) d -> p ct d", p=128), "GW2")
        dma("gpsimd", wo16[:].rearrange("p (dt e) -> p dt e", dt=ND),
            d_woT.rearrange("(dt p) e -> p dt e", p=128), "GW3")

        wl = (("LD", ev_consts),)
        op("vector", lambda e: e.tensor_scalar(out=tdq[:], in0=tdq[:], scalar1=-10000.0,
                                               scalar2=10000.0, op0=ALU.add, op1=ALU.add), waits=wl)
        op("vector", lambda e: e.tensor_scalar(out=tfq[:], in0=tfq[:], scalar1=-10000.0,
                                               scalar2=10000.0, op0=ALU.add, op1=ALU.add))
        op("vector", lambda e: e.tensor_scalar(out=mneg[:], in0=tdq[:], scalar1=-1.0,
                                               scalar2=0.0, op0=ALU.mult, op1=ALU.min))
        op("vector", lambda e: e.tensor_scalar(out=e2b[:], in0=tdq[:], scalar1=0.0,
                                               scalar2=None, op0=ALU.min))
        op("vector", lambda e: e.tensor_scalar(out=e1[:], in0=tfq[:], scalar1=-1.0,
                                               scalar2=0.0, op0=ALU.mult, op1=ALU.min))
        ev_coef = op("vector", lambda e: e.tensor_scalar(out=e2[:], in0=tfq[:], scalar1=0.0,
                                                         scalar2=None, op0=ALU.min), sem_="DV")

        wc = (("DV", ev_coef),)
        op("scalar", lambda e: e.activation(out=e1b[:], in_=mneg[:], func=AF.Exp), waits=wc)
        op("scalar", lambda e: e.activation(out=e2b[:], in_=e2b[:], func=AF.Exp))
        op("scalar", lambda e: e.activation(out=e1[:], in_=e1[:], func=AF.Exp))
        op("scalar", lambda e: e.activation(out=e2[:], in_=e2[:], func=AF.Exp))
        ev_exps = op("scalar", lambda e: e.activation(out=dd[:], in_=mneg[:], func=AF.Exp,
                                                      scale=512.0), sem_="AC")

        we = (("AC", ev_exps),)
        op("vector", lambda e: e.tensor_mul(e12b[:], e1[:], e2b[:]), waits=we)
        op("vector", lambda e: e.tensor_scalar(out=s1c[:], in0=e2[:], scalar1=1e-6,
                                               scalar2=None, op0=ALU.add))
        gt3 = gt[:].rearrange("p (u c) -> p u c", u=8)
        dd3 = dd[:].unsqueeze(1).broadcast_to([128, 8, ND])
        gs13 = gs1[:].rearrange("p (u c) -> p u c", u=8)
        gs23 = gs2[:].rearrange("p (u c) -> p u c", u=8)
        op("vector", lambda e: e.tensor_tensor(out=gt3, in0=dd3, in1=gs13, op=ALU.mult))
        ev_post = op("vector", lambda e: e.tensor_add(gt3, gt3, gs23), sem_="DV")

        for dt in range(ND):
            op("vector", lambda e, dt=dt: e.tensor_tensor_scan(
                out=rcpd[:, dt * TC + 1:(dt + 1) * TC],
                data0=e1b[:, dt:dt + 1].broadcast_to([128, TC - 1]),
                data1=e12b[:, dt:dt + 1].broadcast_to([128, TC - 1]),
                initial=0.0, op0=ALU.mult, op1=ALU.add))
            op("vector", lambda e, dt=dt: e.scalar_tensor_tensor(
                out=sbcol[:, dt:dt + 1], in0=rcpd[:, (dt + 1) * TC - 1:(dt + 1) * TC],
                scalar=e1b[:, dt:dt + 1], in1=e12b[:, dt:dt + 1],
                op0=ALU.mult, op1=ALU.add))
        op("vector", lambda e: e.tensor_scalar(out=cyb[:], in0=sbcol[:],
                                               scalar1=mskt[:, 0:1], scalar2=None, op0=ALU.mult))
        for j in range(1, 8):
            op("vector", lambda e, j=j: e.tensor_mul(cyb[:], cyb[:], gt[:, j * 8:(j + 1) * 8]))
            op("vector", lambda e, j=j: e.scalar_tensor_tensor(
                out=cyb[:], in0=sbcol[:], scalar=mskt[:, j:j + 1], in1=cyb[:],
                op0=ALU.mult, op1=ALU.add))
        ev_cyb = op("vector", lambda e: e.engine_nop(), sem_="DV")

        ev_powt = None
        for dt in range(ND):
            ev_powt = op("scalar", lambda e, dt=dt: e.activation(
                out=powb[:, dt * TC:(dt + 1) * TC], in_=iot, func=AF.Exp,
                scale=mneg[:, dt:dt + 1]),
                waits=(("DV", ev_cyb),) if dt == 0 else ())
        ev_denshift = [None] * ND
        for dt in range(ND):
            op("vector", lambda e, dt=dt: e.scalar_tensor_tensor(
                out=rcpd[:, dt * TC + 1:(dt + 1) * TC], in0=powb[:, dt * TC + 1:(dt + 1) * TC],
                scalar=cyb[:, dt:dt + 1], in1=rcpd[:, dt * TC + 1:(dt + 1) * TC],
                op0=ALU.mult, op1=ALU.add), waits=(("AC", ev_powt),) if dt == 0 else ())
            op("vector", lambda e, dt=dt: e.tensor_copy(
                out=rcpd[:, dt * TC:dt * TC + 1], in_=cyb[:, dt:dt + 1]))
            ev_denshift[dt] = op("vector", lambda e, dt=dt: e.tensor_scalar(
                out=rcpd[:, dt * TC:(dt + 1) * TC], in0=rcpd[:, dt * TC:(dt + 1) * TC],
                scalar1=s1c[:, dt:dt + 1], scalar2=None, op0=ALU.add), sem_="DV")

        op("scalar", lambda e: e.activation(out=rcpd[:], in_=rcpd[:], func=AF.Ln),
           waits=(("DV", ev_denshift[ND - 1]),))
        ev_rcpd = op("scalar", lambda e: e.activation(out=rcpd[:], in_=rcpd[:],
                                                      func=AF.Exp, scale=-1.0), sem_="AC")

        ev_unp = [None] * ND
        ev_pk = [None] * ND
        for it in range(ND):
            w_pk = [] if it == 0 else [("DV", ev_unp[it - 1])]
            ev_pk[it] = dma("sync", sint[:, 0:256], d_pkT[128 * it:128 * (it + 1), :],
                            "LD", waits=tuple(w_pk))
            for h in range(4):
                op("vector", lambda e, h=h: e.tensor_scalar(
                    out=sint[:, 256:512], in0=sint[:, 0:256], scalar1=2 * h,
                    scalar2=3, op0=ALU.logical_shift_right, op1=ALU.bitwise_and),
                    waits=(("LD", ev_pk[it]),) if h == 0 else ())
                op("vector", lambda e: e.tensor_copy(out=sf32[:, 0:256],
                                                     in_=sint[:, 256:512]))
                ev = op("vector", lambda e, it=it, h=h: e.tensor_scalar(
                    out=wt16[:, it * C + 256 * h:it * C + 256 * (h + 1)],
                    in0=sf32[:, 0:256], scalar1=1.0, scalar2=None,
                    op0=ALU.subtract), sem_="DV" if h == 3 else None)
            ev_unp[it] = ev
        ev_bias = dma("sync", sf32[:], d_biast, "LD", waits=(("DV", ev_unp[ND - 1]),))

        ev_vmm = [[None] * ND for _ in range(B)]
        ev_evac = [[None] * ND for _ in range(B)]
        ev_p1 = [[None] * ND for _ in range(B)]
        ev_rmm = [[None] * ND for _ in range(B)]
        ev_rev = [[None] * ND for _ in range(B)]
        ev_pow = [[None] * ND for _ in range(B)]
        ev_corr = [[None] * ND for _ in range(B)]
        ev_ci = [None] * B
        ev_cg = [None] * B
        ev_cya = [None] * B
        ev_rr = [None] * B
        ev_rwkv = [None] * B
        ev_omm = [[[None] * 2 for _ in range(NT)] for _ in range(B)]
        ev_y1 = [[[None] * 2 for _ in range(NT)] for _ in range(B)]
        ev_st1 = [[None] * NT for _ in range(B)]
        ev_ch1 = [None] * B
        ev_xq = [[None] * NT for _ in range(B)]
        ev_x16 = [[None] * NT for _ in range(B)]
        ev_tp = [[None] * ND for _ in range(B)]
        ev_tev = [[None] * ND for _ in range(B)]
        ev_fmm = [[[None] * 2 for _ in range(NT)] for _ in range(B)]
        ev_y2 = [[[None] * 2 for _ in range(NT)] for _ in range(B)]
        ev_st2 = [[None] * NT for _ in range(B)]
        ev_ch2 = [None] * B
        ev_out = [[None] * NT for _ in range(B)]
        ev_st = [[None] * NT for _ in range(B)]
        ev_gx = [16, 0, 0, 0]
        xtk_last = [None]
        out_last = [None, None]

        def nh(b):
            return (b % 2) * ND * TC

        def xh(b):
            return (b % 2) * ND * TC

        psv_last = [None, None]

        def pe_v(b, lo=0, hi=ND, first_extra=()):
            for dt in range(lo, hi):
                _pe_v_dt(b, dt, first_extra if dt == lo else ())
                _act_evacs_dt(b, dt)

        def _pe_v_dt(b, dt, extra):
            if True:
                w = [("GX", ev_gx[b])] + list(extra)
                if b == 0 and dt == 0:
                    w.append(("GW1", 16))
                if psv_last[dt % 2] is not None:
                    w.append(("AC", psv_last[dt % 2]))
                ev = None
                for ct in range(ND):
                    ev = op("tensor", lambda e, b=b, dt=dt, ct=ct: e.matmul(
                        ps_v[dt % 2][:], wv16[:, ct * C + dt * 128:ct * C + (dt + 1) * 128],
                        xfm16[:, xh(b) + ct * TC:xh(b) + (ct + 1) * TC],
                        start=(ct == 0), stop=(ct == ND - 1)),
                        waits=tuple(w) if ct == 0 else (),
                        sem_="PE" if ct == ND - 1 else None)
                ev_vmm[b][dt] = ev

        def pe_r(b):
            for dt in range(ND):
                _pe_r_dt(b, dt)
                _act_rev_dt(b, dt)

        def _pe_r_dt(b, dt):
            if True:
                w = []
                if psv_last[dt % 2] is not None:
                    w.append(("AC", psv_last[dt % 2]))
                if b == 0 and dt == 0:
                    w.append(("GW2", 16))
                ev = None
                for ct in range(ND):
                    ev = op("tensor", lambda e, b=b, dt=dt, ct=ct: e.matmul(
                        ps_v[dt % 2][:], wr16[:, ct * C + dt * 128:ct * C + (dt + 1) * 128],
                        xfm16[:, xh(b) + ct * TC:xh(b) + (ct + 1) * TC],
                        start=(ct == 0), stop=(ct == ND - 1)),
                        waits=tuple(w) if ct == 0 else (),
                        sem_="PE" if ct == ND - 1 else None)
                ev_rmm[b][dt] = ev

        def pe_out(b):
            for tc in range(NT):
                for eh in range(2):
                    w = [("DV", ev_rwkv[b])]
                    if b == 0 and tc == 0 and eh == 0:
                        w.append(("GW3", 16))
                    if tc > 0:
                        w.append(("AC", lambda b=b, tc=tc, eh=eh: ev_y1[b][tc - 1][eh]))
                    elif b > 0:
                        w.append(("AC", ev_y1[b - 1][NT - 1][eh]))
                    ev = None
                    for dt in range(ND):
                        ev = op("tensor", lambda e, dt=dt, tc=tc, eh=eh: e.matmul(
                            ps_o[eh][:], rw16[:, dt * TC + tc * 128:dt * TC + (tc + 1) * 128],
                            wo16[:, dt * C + eh * TC:dt * C + (eh + 1) * TC],
                            start=(dt == 0), stop=(dt == ND - 1)),
                            waits=tuple(w) if dt == 0 else (),
                            sem_="PE" if dt == ND - 1 else None)
                    ev_omm[b][tc][eh] = ev

        def pe_tp(b):
            for it in range(ND):
                _pe_tp_it(b, it)
                _act_tev_it(b, it)

        def _pe_tp_it(b, it):
            if True:
                w_tp = [("DV", ev_x16[b][NT - 1])] if it == 0 else []
                if b == 0 and it == 0:
                    w_tp.append(("LD", ev_consts))
                if it >= 2:
                    w_tp.append(("AC", ev_tev[b][it - 2]))
                elif b > 0:
                    w_tp.append(("AC", ev_tev[b - 1][it + 6]))
                ev = None
                for tc in range(NT):
                    ev = op("tensor", lambda e, it=it, tc=tc: e.transpose(
                        ps_t[it % 2][:, tc * 128:(tc + 1) * 128],
                        rw16[:, tc * C + it * 128:tc * C + (it + 1) * 128], idn[:]),
                        waits=tuple(w_tp) if tc == 0 else (),
                        sem_="PE" if tc == NT - 1 else None)
                ev_tp[b][it] = ev

        def pe_ffn(b):
            for tc in range(NT):
                for oh in range(2):
                    w = [("AC", ev_tev[b][ND - 1])]
                    if b == 0 and tc == 0 and oh == 0:
                        w.append(("DV", ev_unp[ND - 1]))
                        w.append(("LD", ev_bias))
                    if tc > 0:
                        w.append(("DV", lambda b=b, tc=tc, oh=oh: ev_y2[b][tc - 1][oh]))
                    elif b > 0:
                        w.append(("DV", ev_y2[b - 1][NT - 1][oh]))
                    ev = None
                    for it in range(ND):
                        ev = op("tensor", lambda e, it=it, tc=tc, oh=oh: e.matmul(
                            ps_f[oh][:], xqT16[:, it * TC + tc * 128:it * TC + (tc + 1) * 128],
                            wt16[:, it * C + oh * TC:it * C + (oh + 1) * TC],
                            start=(it == 0), stop=(it == ND - 1)),
                            waits=tuple(w) if it == 0 else (),
                            sem_="PE" if it == ND - 1 else None)
                    ev_fmm[b][tc][oh] = ev

        def _act_evacs_dt(b, dt):
            if True:
                w_ev = [("PE", ev_vmm[b][dt])]
                if dt == 0 and b >= 2:
                    w_ev.append(("DV", ev_rwkv[b - 2]))
                if dt < 2 and b > 0:
                    w_ev.append(("DV", ev_p1[b - 1][dt + 6]))
                op("scalar", lambda e, b=b, dt=dt: e.activation(
                    out=num[:, nh(b) + dt * TC:nh(b) + (dt + 1) * TC],
                    in_=ps_v[dt % 2][:], func=AF.Copy,
                    scale=e2[:, dt:dt + 1]), waits=tuple(w_ev))
                w2 = []
                if dt >= 2:
                    w2.append(("DV", lambda b=b, dt=dt: ev_p1[b][dt - 2]))
                ev_evac[b][dt] = op("scalar", lambda e, dt=dt: e.activation(
                    out=scn[:, (dt % 2) * TC:(dt % 2 + 1) * TC], in_=ps_v[dt % 2][:],
                    func=AF.Copy, scale=e12b[:, dt:dt + 1]), waits=tuple(w2))
                psv_last[dt % 2] = ev_evac[b][dt]

        def _act_rev_dt(b, dt):
            if True:
                w = [("PE", ev_rmm[b][dt])]
                if dt == 0 and b > 0:
                    w.append(("PE", ev_tp[b - 1][ND - 1]))
                ev_rev[b][dt] = op("scalar", lambda e, b=b, dt=dt: e.activation(
                    out=rw16[:, dt * TC:(dt + 1) * TC], in_=ps_v[dt % 2][:],
                    func=AF.Tanh, scale=0.5), waits=tuple(w))
                psv_last[dt % 2] = ev_rev[b][dt]

        def act_st1(b):
            for tc in range(NT):
                op("scalar", lambda e, tc=tc: e.activation(
                    out=xqi8[:].bitcast(F32), in_=y1b[:, tc * C:(tc + 1) * C], func=AF.Copy,
                    accum_out=stat[:, tc:tc + 1]), waits=(("XT0", ev_acc[b][tc]),))
                ev_st1[b][tc] = op("scalar", lambda e, tc=tc: e.activation(
                    out=xqi8[:].bitcast(F32), in_=y1b[:, tc * C:(tc + 1) * C], func=AF.Square,
                    accum_out=stat[:, 4 + tc:5 + tc]))

        def act_u(b):
            for tc in range(NT):
                w_xq = [("DV", ev_ch1[b])]
                if tc == 0 and out_last[1] is not None:
                    w_xq.append(("ST0", out_last[1]))
                if tc > 0:
                    w_xq.append(("DV", lambda b=b, tc=tc: ev_xq[b][tc - 1]))
                ev = op("scalar", lambda e, tc=tc: e.activation(
                    out=denb[:], in_=y1b[:, tc * C:(tc + 1) * C],
                    func=AF.Identity, scale=sclq[:, tc:tc + 1], bias=bisq[:, tc:tc + 1]),
                    waits=tuple(w_xq))
                ev_u[b][tc] = ev

        def _act_tev_it(b, it):
            if True:
                w_tev = [("PE", ev_tp[b][it])]
                if b > 0 and it == 0:
                    w_tev.append(("PE", ev_fmm[b - 1][NT - 1][1]))
                ev_tev[b][it] = op("scalar", lambda e, it=it: e.activation(
                    out=xqT16[:, it * TC:(it + 1) * TC], in_=ps_t[it % 2][:],
                    func=AF.Copy), waits=tuple(w_tev))

        def act_st2(b):
            for tc in range(NT):
                op("scalar", lambda e, tc=tc: e.activation(
                    out=xqi8[:].bitcast(F32), in_=y1b[:, tc * C:(tc + 1) * C], func=AF.Copy,
                    accum_out=stat[:, 16 + tc:17 + tc]), waits=(("DV", ev_y2[b][tc][1]),))
                ev_st2[b][tc] = op("scalar", lambda e, tc=tc: e.activation(
                    out=xqi8[:].bitcast(F32), in_=y1b[:, tc * C:(tc + 1) * C], func=AF.Square,
                    accum_out=stat[:, 20 + tc:21 + tc]))

        def act_out(b):
            for tc in range(NT):
                buf = outb if tc % 2 == 0 else denb
                w_o = [("DV", ev_ch2[b])]
                if out_last[tc % 2] is not None:
                    w_o.append(("ST0", out_last[tc % 2]))
                ev_out[b][tc] = op("scalar", lambda e, tc=tc, buf=buf: e.activation(
                    out=buf[:], in_=y1b[:, tc * C:(tc + 1) * C], func=AF.Identity,
                    scale=inv2[:, tc:tc + 1], bias=bis2[:, tc:tc + 1]),
                    waits=tuple(w_o))
                ev_st[b][tc] = dma("sync", d_y[b, tc * 128:(tc + 1) * 128, :],
                                   buf[:], "ST0", waits=(("AC", ev_out[b][tc]),))
                out_last[tc % 2] = ev_st[b][tc]

        def v_p1(b):
            for dt in range(ND):
                w_sc = [("AC", ev_evac[b][dt])]
                if b == 0 and dt == 0:
                    w_sc.append(("DV", ev_post))
                if dt == 0:
                    w_sc.append(("CI", 16 * b))
                h = (dt % 2) * TC
                op("vector", lambda e, dt=dt, h=h: e.tensor_tensor_scan(
                    out=scn[:, h:h + TC],
                    data0=e1b[:, dt:dt + 1].broadcast_to([128, TC]),
                    data1=scn[:, h:h + TC],
                    initial=0.0, op0=ALU.mult, op1=ALU.add), waits=tuple(w_sc))
                op("vector", lambda e, b=b, dt=dt, h=h: e.tensor_add(
                    num[:, nh(b) + dt * TC + 1:nh(b) + (dt + 1) * TC],
                    scn[:, h:h + TC - 1],
                    num[:, nh(b) + dt * TC + 1:nh(b) + (dt + 1) * TC]))
                ev_p1[b][dt] = op("vector", lambda e, dt=dt, h=h: e.tensor_copy(
                    out=scol[:, dt:dt + 1], in_=scn[:, h + TC - 1:h + TC]))

        def v_carry(b):
            op("vector", lambda e: e.tensor_scalar(
                out=cya[:], in0=sg[:, 0:8], scalar1=mskt[:, 0:1], scalar2=None,
                op0=ALU.mult), waits=(("CG", ev_cg[b]),))
            for j in range(1, 8):
                op("vector", lambda e, j=j: e.tensor_mul(cya[:], cya[:], gt[:, j * 8:(j + 1) * 8]))
                op("vector", lambda e, j=j: e.scalar_tensor_tensor(
                    out=cya[:], in0=sg[:, j * 8:(j + 1) * 8], scalar=mskt[:, j:j + 1],
                    in1=cya[:], op0=ALU.mult, op1=ALU.add))
            ev_cya[b] = op("vector", lambda e: e.engine_nop())

        def v_corrwkv(b):
            for dt in range(ND):
                ev_corr[b][dt] = op("vector", lambda e, b=b, dt=dt: e.scalar_tensor_tensor(
                    out=num[:, nh(b) + dt * TC:nh(b) + (dt + 1) * TC],
                    in0=powb[:, dt * TC:(dt + 1) * TC],
                    scalar=cya[:, dt:dt + 1],
                    in1=num[:, nh(b) + dt * TC:nh(b) + (dt + 1) * TC],
                    op0=ALU.mult, op1=ALU.add))
            w_wkv = []
            if b == 0:
                w_wkv.append(("AC", ev_rcpd))
            op("vector", lambda e, b=b: e.tensor_mul(
                num[:, nh(b):nh(b) + ND * TC], num[:, nh(b):nh(b) + ND * TC], rcpd[:]),
                waits=tuple(w_wkv))

        def v_rwkv(b):
            for dt in range(ND):
                ev = op("vector", lambda e, b=b, dt=dt: e.tensor_mul(
                    rw16[:, dt * TC:(dt + 1) * TC],
                    num[:, nh(b) + dt * TC:nh(b) + (dt + 1) * TC],
                    rw16[:, dt * TC:(dt + 1) * TC]),
                    waits=(("GP", ev_rr[b] - ND + 1 + dt),))
            ev_rwkv[b] = ev

        ev_acc = [[None] * NT for _ in range(B)]
        ev_mxmn = [None] * B

        def v_y1(b):
            for tc in range(NT):
                for eh in range(2):
                    w_y1 = [("PE", ev_omm[b][tc][eh])]
                    if b > 0 and eh == 0:
                        w_y1.append(("AC", ev_out[b - 1][tc]))
                    ev_y1[b][tc][eh] = op("scalar", lambda e, b=b, tc=tc, eh=eh: e.activation(
                        out=y1b[:, tc * C + eh * TC:tc * C + (eh + 1) * TC],
                        in_=ps_o[eh][:], func=AF.Copy), waits=tuple(w_y1))
                ev_acc[b][tc] = S.emit(
                    "gpsimd",
                    lambda e, b=b, tc=tc: e.dma_start(
                        out=y1b[:, tc * C:(tc + 1) * C],
                        in_=d_xtok[b, tc * 128:(tc + 1) * 128, :],
                        accum_op=ALU.add),
                    waits=(("AC", ev_y1[b][tc][1]),), inc=("XT0", 16))
                op("vector", lambda e, tc=tc: e.tensor_reduce(
                    out=stat[:, 8 + tc:9 + tc], in_=y1b[:, tc * C:(tc + 1) * C],
                    axis=AX.X, op=ALU.max), waits=(("XT0", ev_acc[b][tc]),))
                ev = op("vector", lambda e, tc=tc: e.tensor_reduce(
                    out=stat[:, 12 + tc:13 + tc], in_=y1b[:, tc * C:(tc + 1) * C],
                    axis=AX.X, op=ALU.min))
            ev_mxmn[b] = ev

        def v_chain1(b):
            op("vector", lambda e: e.tensor_scalar(
                out=t0c[:], in0=stat[:, 0:4], scalar1=1.0 / C, scalar2=None,
                op0=ALU.mult), waits=(("AC", ev_st1[b][NT - 1]), ("DV", ev_mxmn[b])))
            op("vector", lambda e: e.tensor_mul(t1c[:], t0c[:], t0c[:]))
            op("vector", lambda e: e.scalar_tensor_tensor(
                out=t1c[:], in0=stat[:, 4:8], scalar=1.0 / C, in1=t1c[:],
                op0=ALU.mult, op1=ALU.subtract))
            op("vector", lambda e: e.tensor_scalar(
                out=t1c[:], in0=t1c[:], scalar1=1e-5, scalar2=None, op0=ALU.add))
            op("vector", lambda e: e.tensor_scalar(
                out=inv1[:].bitcast(I32), in0=t1c[:].bitcast(I32), scalar1=1,
                scalar2=None, op0=ALU.logical_shift_right))
            op("vector", lambda e: e.tensor_scalar(
                out=inv1[:].bitcast(I32), in0=inv1[:].bitcast(I32), scalar1=-1,
                scalar2=RSQRT_MAGIC, op0=ALU.mult, op1=ALU.add))
            for _ in range(2):
                op("vector", lambda e: e.tensor_mul(t2c[:], t1c[:], inv1[:]))
                op("vector", lambda e: e.tensor_mul(t2c[:], t2c[:], inv1[:]))
                op("vector", lambda e: e.tensor_scalar(
                    out=t2c[:], in0=t2c[:], scalar1=-0.5, scalar2=1.5,
                    op0=ALU.mult, op1=ALU.add))
                op("vector", lambda e: e.tensor_mul(inv1[:], inv1[:], t2c[:]))
            op("vector", lambda e: e.tensor_sub(t2c[:], stat[:, 8:12], t0c[:]))
            op("vector", lambda e: e.tensor_sub(t1c[:], t0c[:], stat[:, 12:16]))
            op("vector", lambda e: e.tensor_max(t2c[:], t2c[:], t1c[:]))
            op("vector", lambda e: e.tensor_mul(t2c[:], t2c[:], inv1[:]))
            op("vector", lambda e: e.tensor_scalar(
                out=t2c[:], in0=t2c[:], scalar1=1e-5, scalar2=None, op0=ALU.max))
            op("vector", lambda e: e.tensor_scalar(
                out=wsoa[:], in0=t2c[:], scalar1=wsc[:], scalar2=1.0 / 127.0,
                op0=ALU.mult, op1=ALU.mult))
            op("vector", lambda e: e.reciprocal(out=sclq[:], in_=t2c[:]))
            op("vector", lambda e: e.tensor_scalar(
                out=sclq[:], in0=sclq[:], scalar1=127.0, scalar2=None, op0=ALU.mult))
            op("vector", lambda e: e.tensor_mul(sclq[:], sclq[:], inv1[:]))
            op("vector", lambda e: e.tensor_mul(bisq[:], t0c[:], sclq[:]))
            ev_ch1[b] = op("vector", lambda e: e.tensor_scalar(
                out=bisq[:], in0=bisq[:], scalar1=-1.0, scalar2=None, op0=ALU.mult))

        def v_quant(b):
            for tc in range(NT):
                ev_xq[b][tc] = op("vector", lambda e, tc=tc: e.tensor_scalar(
                    out=xqi8[:, tc * C:(tc + 1) * C], in0=denb[:], scalar1=12582912.0,
                    scalar2=-12582912.0, op0=ALU.add, op1=ALU.add),
                    waits=(("AC", ev_u[b][tc]),))

        def v_y2(b):
            for tc in range(NT):
                for oh in range(2):
                    op("vector", lambda e, b=b, tc=tc, oh=oh: e.scalar_tensor_tensor(
                        out=denb[:, oh * TC:(oh + 1) * TC], in0=ps_f[oh][:],
                        scalar=wsoa[:, tc:tc + 1], in1=sf32[:, oh * TC:(oh + 1) * TC],
                        op0=ALU.mult, op1=ALU.add),
                        waits=(("PE", ev_fmm[b][tc][oh]), ("DV", ev_xq[b][NT - 1])))
                    ev_y2[b][tc][oh] = op("vector", lambda e, tc=tc, oh=oh: e.scalar_tensor_tensor(
                        out=y1b[:, tc * C + oh * TC:tc * C + (oh + 1) * TC],
                        in0=y1b[:, tc * C + oh * TC:tc * C + (oh + 1) * TC],
                        scalar=inv1[:, tc:tc + 1],
                        in1=denb[:, oh * TC:(oh + 1) * TC],
                        op0=ALU.mult, op1=ALU.add))

        def v_chain2(b):
            op("vector", lambda e: e.tensor_scalar(
                out=t0c[:], in0=stat[:, 16:20], scalar1=1.0 / C, scalar2=None,
                op0=ALU.mult), waits=(("AC", ev_st2[b][NT - 1]),))
            op("vector", lambda e: e.tensor_mul(t1c[:], t0c[:], t0c[:]))
            op("vector", lambda e: e.scalar_tensor_tensor(
                out=t1c[:], in0=stat[:, 20:24], scalar=1.0 / C, in1=t1c[:],
                op0=ALU.mult, op1=ALU.subtract))
            op("vector", lambda e: e.tensor_scalar(
                out=t1c[:], in0=t1c[:], scalar1=1e-5, scalar2=None, op0=ALU.add))
            op("vector", lambda e: e.tensor_scalar(
                out=inv2[:].bitcast(I32), in0=t1c[:].bitcast(I32), scalar1=1,
                scalar2=None, op0=ALU.logical_shift_right))
            op("vector", lambda e: e.tensor_scalar(
                out=inv2[:].bitcast(I32), in0=inv2[:].bitcast(I32), scalar1=-1,
                scalar2=RSQRT_MAGIC, op0=ALU.mult, op1=ALU.add))
            for _ in range(3):
                op("vector", lambda e: e.tensor_mul(t2c[:], t1c[:], inv2[:]))
                op("vector", lambda e: e.tensor_mul(t2c[:], t2c[:], inv2[:]))
                op("vector", lambda e: e.tensor_scalar(
                    out=t2c[:], in0=t2c[:], scalar1=-0.5, scalar2=1.5,
                    op0=ALU.mult, op1=ALU.add))
                op("vector", lambda e: e.tensor_mul(inv2[:], inv2[:], t2c[:]))
            op("vector", lambda e: e.tensor_mul(bis2[:], t0c[:], inv2[:]))
            ev_ch2[b] = op("vector", lambda e: e.tensor_scalar(
                out=bis2[:], in0=bis2[:], scalar1=-1.0, scalar2=None, op0=ALU.mult))

        def g_gather(b):
            ev_ci[b] = dma("sync", d_ccin[b].rearrange("(p f) -> p f", p=128), scol[:],
                           "CI", waits=(("DV", ev_p1[b][ND - 1]),))
            S.emit("gpsimd", lambda e, b=b: e.collective_compute(
                "AllGather", ALU.bypass, ins=[d_ccin[b]], outs=[d_ccout[b]],
                replica_groups=[[0, 1, 2, 3, 4, 5, 6, 7]]),
                waits=(("CI", ev_ci[b]),), inc=("CC", 1))
            w_sg = [("CC", b + 1)]
            if b > 0:
                w_sg.append(("DV", ev_cya[b - 1]))
            ev_cg[b] = dma("gpsimd", sg[:].rearrange("p (j f) -> p j f", j=8),
                           d_ccout[b].rearrange("j (p f) -> p j f", p=128), "CG",
                           waits=tuple(w_sg))

        def g_rr(b):
            for dt in range(ND):
                ev_rr[b] = S.emit("gpsimd", lambda e, dt=dt: e.tensor_scalar(
                    out=rw16[:, dt * TC:(dt + 1) * TC], in0=rw16[:, dt * TC:(dt + 1) * TC],
                    scalar1=0.5, scalar2=0.5, op0=ALU.mult, op1=ALU.add),
                    waits=(("AC", ev_rev[b][dt]),))

        def v_x16(b):
            for tc in range(NT):
                w = []
                if tc == 0:
                    w.append(("PE", ev_omm[b][NT - 1][1]))
                ev_x16[b][tc] = op("vector", lambda e, tc=tc: e.tensor_copy(
                    out=rw16[:, tc * C:(tc + 1) * C], in_=xqi8[:, tc * C:(tc + 1) * C]),
                    waits=tuple(w))

        def g_xfm(b):
            ev_gx[b] = dma("gpsimd", xfm16[:, xh(b):xh(b) + ND * TC].rearrange(
                "p (ct t) -> p ct t", ct=ND),
                d_xfm[b].rearrange("(ct p) t -> p ct t", p=128), "GX",
                waits=(("PE", ev_rmm[b - 2][ND - 1]),))

        ev_u = [[None] * NT for _ in range(B)]

        pe_v(0)
        v_p1(0)
        g_gather(0)
        pe_r(0)
        g_rr(0)
        ev_gx[1] = dma("gpsimd", xfm16[:, ND * TC:2 * ND * TC].rearrange(
            "p (ct t) -> p ct t", ct=ND),
            d_xfm[1].rearrange("(ct p) t -> p ct t", p=128), "GX")
        pe_v(1)
        v_carry(0)

        v_corrwkv(0)
        for b in range(B):
            v_rwkv(b)
            if b > 0:
                for kv in range(ev_y2[b - 1][0][0], ev_rwkv[b], 4):
                    op("tensor", lambda e: e.matmul(
                        ps_o[0][:, 0:8], idn[:], idn[:, 0:8],
                        start=True, stop=True), waits=(("DV", kv),))
            pe_out(b)
            if b + 2 < B:
                g_xfm(b + 2)
            if b + 1 < B:
                v_p1(b + 1)
            v_y1(b)
            if b + 2 < B:
                pe_v(b + 2, 0, 2)
            act_st1(b)
            v_chain1(b)
            act_u(b)
            v_quant(b)
            v_x16(b)
            if b + 1 < B:
                g_gather(b + 1)
                v_carry(b + 1)
                v_corrwkv(b + 1)
            w0 = [("AC", ev_y1[b][NT - 1][0])]
            if b == 0:
                w0.append(("LD", ev_consts))
            for kv in range(ev_mxmn[b] + 2, ev_x16[b][NT - 1], 3):
                op("tensor", lambda e: e.matmul(
                    ps_o[0][:, 0:8], idn[:], idn[:, 0:8],
                    start=True, stop=True), waits=tuple([("DV", kv)] + w0))
                w0 = []
            pe_tp(b)
            if b + 1 < B:
                pe_r(b + 1)
                g_rr(b + 1)
            pe_ffn(b)
            v_y2(b)
            act_st2(b)
            v_chain2(b)
            act_out(b)
            if b + 2 < B:
                pe_v(b + 2, 2, ND)

        S.emit("sync", None, waits=(("ST0", S.cnt["ST0"]),))

        S.check()

        with nc.Block() as block:
            def runner(eng_name):
                def body(eng):
                    for fn, waits, inc in S.ops[eng_name]:
                        for s, v in waits:
                            eng.wait_ge(sems[s], v() if callable(v) else v)
                        if fn is not None:
                            instr = fn(eng)
                            if inc is not None:
                                instr.then_inc(sems[inc[0]], inc[1])
                return body
            block.sync(runner("sync"))
            block.scalar(runner("scalar"))
            block.vector(runner("vector"))
            block.tensor(runner("tensor"))
            block.gpsimd(runner("gpsimd"))

    return nc


_NC_CACHE = None


def kernel(L0_attn_x, L0_attn_time_decay, L0_attn_time_first, L0_attn_key_w,
           L0_attn_value_w, L0_attn_recep_w, L0_attn_out_w, L0_ffn_packed_weight,
           L0_ffn_weight_scale, L0_ffn_bias):
    global _NC_CACHE
    x = np.asarray(L0_attn_x, dtype=np.float32)
    td = np.asarray(L0_attn_time_decay, dtype=np.float32)
    tf = np.asarray(L0_attn_time_first, dtype=np.float32)
    wv = np.asarray(L0_attn_value_w, dtype=np.float32)
    wr = np.asarray(L0_attn_recep_w, dtype=np.float32)
    wo = np.asarray(L0_attn_out_w, dtype=np.float32)
    pk = np.asarray(L0_ffn_packed_weight, dtype=np.int32)
    ws = np.asarray(L0_ffn_weight_scale, dtype=np.float32)
    bi = np.asarray(L0_ffn_bias, dtype=np.float32)

    wvT = np.ascontiguousarray(wv.T)
    wrT = np.ascontiguousarray(wr.T)
    woT = np.ascontiguousarray(wo.T)
    pkT = np.ascontiguousarray(pk.reshape(256, C).T)
    tdt = np.ascontiguousarray(td.reshape(ND, 128).T)
    tft = np.ascontiguousarray(tf.reshape(ND, 128).T)
    iota = np.ascontiguousarray(np.broadcast_to(
        np.arange(TC, dtype=np.float32), (128, TC)))
    wscol = np.ascontiguousarray(np.broadcast_to(ws.reshape(1, 1), (128, 1)))
    ident = np.eye(128, dtype=ml_dtypes.bfloat16)
    biast = np.ascontiguousarray(np.broadcast_to(bi.reshape(1, C), (128, C)))

    in_maps = []
    for k in range(NCORES):
        sl = x[:, TC * k:TC * (k + 1), :]
        maskt = np.ascontiguousarray(np.broadcast_to(
            (np.arange(8) < k).astype(np.float32), (128, 8)))
        gsel = np.ascontiguousarray(np.broadcast_to(
            np.repeat((np.arange(8) < k).astype(np.float32), 8), (128, 64)))
        in_maps.append(dict(
            xfm=np.ascontiguousarray(sl.transpose(0, 2, 1)),
            xtok=np.ascontiguousarray(sl),
            wvT=wvT, wrT=wrT, woT=woT, pkT=pkT, tdt=tdt, tft=tft, iota=iota,
            maskt=maskt, gsel=gsel, gselc=np.ascontiguousarray(1.0 - gsel),
            wscol=wscol, ident=ident, biast=biast,
        ))

    if _NC_CACHE is None:
        _NC_CACHE = build_nc()
    res = run_bass_kernel_spmd(_NC_CACHE, in_maps, core_ids=list(range(NCORES)))
    out = np.empty((B, NCORES * TC, C), dtype=np.float32)
    for k in range(NCORES):
        out[:, TC * k:TC * (k + 1), :] = res.results[k]["y"]
    return out


# version 20
# speedup vs baseline: 1.2553x; 1.0032x over previous
import contextlib
import numpy as np
import ml_dtypes
import concourse.bass as bass
import concourse.mybir as mybir
from concourse.bass_utils import run_bass_kernel_spmd

F32 = mybir.dt.float32
BF16 = mybir.dt.bfloat16
I8 = mybir.dt.int8
I32 = mybir.dt.int32
AF = mybir.ActivationFunctionType
ALU = mybir.AluOpType
AX = mybir.AxisListType

NCORES = 8
B = 4
TC = 512
NT = 4
ND = 8
C = 1024
DEBUG_DUMPS = False
RSQRT_MAGIC = 0x5F3759DF
RCP_MAGIC = 0x7EF311C3


class Sched:

    def __init__(self):
        self.ops = {k: [] for k in ("sync", "scalar", "vector", "tensor", "gpsimd")}
        self.cnt = {}

    AUTO = {"vector": "DV", "scalar": "AC", "gpsimd": "GP"}

    def emit(self, eng, fn, waits=(), inc=None):
        waits = list(waits)
        if fn is not None and eng in self.AUTO and inc is None:
            seq = self.AUTO[eng]
            prev = self.cnt.get(seq, 0)
            if prev > 0:
                waits.append((seq, prev))
            inc = (seq, 1)
        if inc is not None:
            self.cnt[inc[0]] = self.cnt.get(inc[0], 0) + inc[1]
        self.ops[eng].append((fn, tuple(waits), inc))
        return self.cnt.get(inc[0]) if inc is not None else None

    def check(self):
        pos = {k: 0 for k in self.ops}
        sem = {}
        progress = True
        while progress:
            progress = False
            for eng, lst in self.ops.items():
                while pos[eng] < len(lst):
                    fn, waits, inc = lst[pos[eng]]
                    if all(sem.get(s, 0) >= (v() if callable(v) else v)
                           for s, v in waits):
                        if inc is not None:
                            sem[inc[0]] = sem.get(inc[0], 0) + inc[1]
                        pos[eng] += 1
                        progress = True
                    else:
                        break
        stuck = {e: p for e, p in pos.items() if p < len(self.ops[e])}
        if stuck:
            msgs = []
            for e, p in stuck.items():
                _, waits, _ = self.ops[e][p]
                unmet = [(s, (v() if callable(v) else v), sem.get(s, 0)) for s, v in waits
                         if sem.get(s, 0) < (v() if callable(v) else v)]
                msgs.append(f"{e} op {p}/{len(self.ops[e])} waits {unmet}")
            raise RuntimeError("schedule deadlock: " + "; ".join(msgs))


def build_nc():
    nc = bass.Bass("TRN2", target_bir_lowering=False,
                   detect_race_conditions=False)

    d_xfm = nc.dram_tensor("xfm", [B, C, TC], F32, kind="ExternalInput").ap()
    d_xtok = nc.dram_tensor("xtok", [B, TC, C], F32, kind="ExternalInput").ap()
    d_wvT = nc.dram_tensor("wvT", [C, C], F32, kind="ExternalInput").ap()
    d_wrT = nc.dram_tensor("wrT", [C, C], F32, kind="ExternalInput").ap()
    d_woT = nc.dram_tensor("woT", [C, C], F32, kind="ExternalInput").ap()
    d_pkT = nc.dram_tensor("pkT", [1024, 256], I32, kind="ExternalInput").ap()
    d_tdt = nc.dram_tensor("tdt", [128, ND], F32, kind="ExternalInput").ap()
    d_tft = nc.dram_tensor("tft", [128, ND], F32, kind="ExternalInput").ap()
    d_iota = nc.dram_tensor("iota", [128, TC], F32, kind="ExternalInput").ap()
    d_mask = nc.dram_tensor("maskt", [128, 8], F32, kind="ExternalInput").ap()
    d_gsel = nc.dram_tensor("gsel", [128, 64], F32, kind="ExternalInput").ap()
    d_gselc = nc.dram_tensor("gselc", [128, 64], F32, kind="ExternalInput").ap()
    d_wscol = nc.dram_tensor("wscol", [128, 1], F32, kind="ExternalInput").ap()
    d_ident = nc.dram_tensor("ident", [128, 128], BF16, kind="ExternalInput").ap()
    d_biast = nc.dram_tensor("biast", [128, C], F32, kind="ExternalInput").ap()
    d_y = nc.dram_tensor("y", [B, TC, C], F32, kind="ExternalOutput").ap()
    d_ccin = nc.dram_tensor("ccin", [B, C], F32, kind="Internal").ap()
    d_ccout = nc.dram_tensor("ccout", [B, 8, C], F32, kind="Internal").ap()

    es = contextlib.ExitStack()

    def sbuf(name, shape, dt):
        return es.enter_context(nc.sbuf_tensor(name, shape, dt))

    def psum(name, shape, dt):
        return es.enter_context(nc.psum_tensor(name, shape, dt))

    with es:
        wv16 = sbuf("wv16", [128, ND * C], BF16)
        wr16 = sbuf("wr16", [128, ND * C], BF16)
        wo16 = sbuf("wo16", [128, ND * C], BF16)
        wt16 = sbuf("wt16", [128, ND * C], BF16)
        rcpd = sbuf("rcpd", [128, ND * TC], F32)
        denb = sbuf("denb", [128, C], F32)
        powb = sbuf("powb", [128, ND * TC], F32)
        sf32 = sbuf("sf32", [128, C], F32)
        xqi8 = sbuf("xqi8", [128, NT * C], I8)
        idn = sbuf("idn", [128, 128], BF16)
        xfm16 = sbuf("xfm16", [128, 2 * ND * TC], BF16)
        xqT16 = sbuf("xqT16", [128, ND * TC], BF16)
        scn = sbuf("scn", [128, 2 * TC], F32)
        num = sbuf("num", [128, 2 * ND * TC], F32)
        rw16 = sbuf("rw16", [128, ND * TC], BF16)
        y1b = sbuf("y1b", [128, NT * C], F32)
        sint = y1b[:, 0:TC].bitcast(I32)
        outb = sbuf("outb", [128, C], F32)
        iot = outb[:, 0:TC]
        tdq = sbuf("tdq", [128, ND], F32)
        tfq = sbuf("tfq", [128, ND], F32)
        mneg = sbuf("mneg", [128, ND], F32)
        e1b = sbuf("e1b", [128, ND], F32)
        e2b = sbuf("e2b", [128, ND], F32)
        e1 = sbuf("e1", [128, ND], F32)
        e2 = sbuf("e2", [128, ND], F32)
        e12b = sbuf("e12b", [128, ND], F32)
        s1c = sbuf("s1c", [128, ND], F32)
        dd = sbuf("dd", [128, ND], F32)
        gt = sbuf("gt", [128, 64], F32)
        mskt = sbuf("mskt", [128, 8], F32)
        gs1 = sbuf("gs1", [128, 64], F32)
        gs2 = sbuf("gs2", [128, 64], F32)
        wsc = sbuf("wsc", [128, 1], F32)
        sbcol = sbuf("sbcol", [128, ND], F32)
        cyb = sbuf("cyb", [128, ND], F32)
        scol = sbuf("scol", [128, ND], F32)
        sg = sbuf("sg", [128, 64], F32)
        cya = sbuf("cya", [128, ND], F32)
        stat = sbuf("stat", [128, 24], F32)
        inv1 = sbuf("inv1", [128, NT], F32)
        sclq = sbuf("sclq", [128, NT], F32)
        bisq = sbuf("bisq", [128, NT], F32)
        wsoa = sbuf("wsoa", [128, NT], F32)
        inv2 = sbuf("inv2", [128, NT], F32)
        bis2 = sbuf("bis2", [128, NT], F32)
        t0c = sbuf("t0c", [128, NT], F32)
        t1c = sbuf("t1c", [128, NT], F32)
        t2c = sbuf("t2c", [128, NT], F32)

        ps_v = [psum("ps_v0", [128, TC], F32), psum("ps_v1", [128, TC], F32)]
        ps_o = [psum("ps_o0", [128, TC], F32), psum("ps_o1", [128, TC], F32)]
        ps_f = [psum("ps_f0", [128, TC], F32), psum("ps_f1", [128, TC], F32)]
        ps_t = [psum("ps_t0", [128, TC], BF16), psum("ps_t1", [128, TC], BF16)]

        semnames = ("LD", "XT0", "XT1", "GX", "GW1", "GW2", "GW3", "CI", "CC",
                    "CG", "PE", "AC", "DV", "GP", "ST0", "ST1")
        sems = {n: es.enter_context(nc.semaphore(n)) for n in semnames}

        S = Sched()

        def dma(eng, dst, src, sem_, waits=()):
            return S.emit(eng, lambda e, d=dst, s=src: e.dma_start(out=d, in_=s),
                          waits=waits, inc=(sem_, 16))

        def op(eng, fn, waits=(), sem_=None):
            return S.emit(eng, fn, waits=waits,
                          inc=(sem_, 1) if (sem_ and eng == "tensor") else None)

        dma("sync", tdq[:], d_tdt, "LD")
        dma("sync", tfq[:], d_tft, "LD")
        dma("sync", iot, d_iota, "LD")
        dma("sync", mskt[:], d_mask, "LD")
        dma("sync", gs1[:], d_gsel, "LD")
        dma("sync", gs2[:], d_gselc, "LD")
        dma("sync", wsc[:], d_wscol, "LD")
        ev_consts = dma("sync", idn[:], d_ident, "LD")

        dma("gpsimd", xfm16[:, 0:ND * TC].rearrange("p (ct t) -> p ct t", ct=ND),
            d_xfm[0].rearrange("(ct p) t -> p ct t", p=128), "GX")
        dma("gpsimd", wv16[:].rearrange("p (ct d) -> p ct d", ct=ND),
            d_wvT.rearrange("(ct p) d -> p ct d", p=128), "GW1")
        dma("gpsimd", wr16[:].rearrange("p (ct d) -> p ct d", ct=ND),
            d_wrT.rearrange("(ct p) d -> p ct d", p=128), "GW2")
        dma("gpsimd", wo16[:].rearrange("p (dt e) -> p dt e", dt=ND),
            d_woT.rearrange("(dt p) e -> p dt e", p=128), "GW3")

        wl = (("LD", ev_consts),)
        op("vector", lambda e: e.tensor_scalar(out=tdq[:], in0=tdq[:], scalar1=-10000.0,
                                               scalar2=10000.0, op0=ALU.add, op1=ALU.add), waits=wl)
        op("vector", lambda e: e.tensor_scalar(out=tfq[:], in0=tfq[:], scalar1=-10000.0,
                                               scalar2=10000.0, op0=ALU.add, op1=ALU.add))
        op("vector", lambda e: e.tensor_scalar(out=mneg[:], in0=tdq[:], scalar1=-1.0,
                                               scalar2=0.0, op0=ALU.mult, op1=ALU.min))
        op("vector", lambda e: e.tensor_scalar(out=e2b[:], in0=tdq[:], scalar1=0.0,
                                               scalar2=None, op0=ALU.min))
        op("vector", lambda e: e.tensor_scalar(out=e1[:], in0=tfq[:], scalar1=-1.0,
                                               scalar2=0.0, op0=ALU.mult, op1=ALU.min))
        ev_coef = op("vector", lambda e: e.tensor_scalar(out=e2[:], in0=tfq[:], scalar1=0.0,
                                                         scalar2=None, op0=ALU.min), sem_="DV")

        wc = (("DV", ev_coef),)
        op("scalar", lambda e: e.activation(out=e1b[:], in_=mneg[:], func=AF.Exp), waits=wc)
        op("scalar", lambda e: e.activation(out=e2b[:], in_=e2b[:], func=AF.Exp))
        op("scalar", lambda e: e.activation(out=e1[:], in_=e1[:], func=AF.Exp))
        op("scalar", lambda e: e.activation(out=e2[:], in_=e2[:], func=AF.Exp))
        ev_exps = op("scalar", lambda e: e.activation(out=dd[:], in_=mneg[:], func=AF.Exp,
                                                      scale=512.0), sem_="AC")

        we = (("AC", ev_exps),)
        op("vector", lambda e: e.tensor_mul(e12b[:], e1[:], e2b[:]), waits=we)
        op("vector", lambda e: e.tensor_scalar(out=s1c[:], in0=e2[:], scalar1=1e-6,
                                               scalar2=None, op0=ALU.add))
        gt3 = gt[:].rearrange("p (u c) -> p u c", u=8)
        dd3 = dd[:].unsqueeze(1).broadcast_to([128, 8, ND])
        gs13 = gs1[:].rearrange("p (u c) -> p u c", u=8)
        gs23 = gs2[:].rearrange("p (u c) -> p u c", u=8)
        op("vector", lambda e: e.tensor_tensor(out=gt3, in0=dd3, in1=gs13, op=ALU.mult))
        ev_post = op("vector", lambda e: e.tensor_add(gt3, gt3, gs23), sem_="DV")

        for dt in range(ND):
            op("vector", lambda e, dt=dt: e.tensor_tensor_scan(
                out=rcpd[:, dt * TC + 1:(dt + 1) * TC],
                data0=e1b[:, dt:dt + 1].broadcast_to([128, TC - 1]),
                data1=e12b[:, dt:dt + 1].broadcast_to([128, TC - 1]),
                initial=0.0, op0=ALU.mult, op1=ALU.add))
            op("vector", lambda e, dt=dt: e.scalar_tensor_tensor(
                out=sbcol[:, dt:dt + 1], in0=rcpd[:, (dt + 1) * TC - 1:(dt + 1) * TC],
                scalar=e1b[:, dt:dt + 1], in1=e12b[:, dt:dt + 1],
                op0=ALU.mult, op1=ALU.add))
        op("vector", lambda e: e.tensor_scalar(out=cyb[:], in0=sbcol[:],
                                               scalar1=mskt[:, 0:1], scalar2=None, op0=ALU.mult))
        for j in range(1, 8):
            op("vector", lambda e, j=j: e.tensor_mul(cyb[:], cyb[:], gt[:, j * 8:(j + 1) * 8]))
            op("vector", lambda e, j=j: e.scalar_tensor_tensor(
                out=cyb[:], in0=sbcol[:], scalar=mskt[:, j:j + 1], in1=cyb[:],
                op0=ALU.mult, op1=ALU.add))
        ev_cyb = op("vector", lambda e: e.engine_nop(), sem_="DV")

        ev_powt = None
        for dt in range(ND):
            ev_powt = op("scalar", lambda e, dt=dt: e.activation(
                out=powb[:, dt * TC:(dt + 1) * TC], in_=iot, func=AF.Exp,
                scale=mneg[:, dt:dt + 1]),
                waits=(("DV", ev_cyb),) if dt == 0 else ())
        ev_denshift = [None] * ND
        for dt in range(ND):
            op("vector", lambda e, dt=dt: e.scalar_tensor_tensor(
                out=rcpd[:, dt * TC + 1:(dt + 1) * TC], in0=powb[:, dt * TC + 1:(dt + 1) * TC],
                scalar=cyb[:, dt:dt + 1], in1=rcpd[:, dt * TC + 1:(dt + 1) * TC],
                op0=ALU.mult, op1=ALU.add), waits=(("AC", ev_powt),) if dt == 0 else ())
            op("vector", lambda e, dt=dt: e.tensor_copy(
                out=rcpd[:, dt * TC:dt * TC + 1], in_=cyb[:, dt:dt + 1]))
            ev_denshift[dt] = op("vector", lambda e, dt=dt: e.tensor_scalar(
                out=rcpd[:, dt * TC:(dt + 1) * TC], in0=rcpd[:, dt * TC:(dt + 1) * TC],
                scalar1=s1c[:, dt:dt + 1], scalar2=None, op0=ALU.add), sem_="DV")

        op("scalar", lambda e: e.activation(out=rcpd[:], in_=rcpd[:], func=AF.Ln),
           waits=(("DV", ev_denshift[ND - 1]),))
        ev_rcpd = op("scalar", lambda e: e.activation(out=rcpd[:], in_=rcpd[:],
                                                      func=AF.Exp, scale=-1.0), sem_="AC")

        ev_unp = [None] * ND
        ev_pk = [None] * ND
        for it in range(ND):
            w_pk = [] if it == 0 else [("DV", ev_unp[it - 1])]
            ev_pk[it] = dma("sync", sint[:, 0:256], d_pkT[128 * it:128 * (it + 1), :],
                            "LD", waits=tuple(w_pk))
            for h in range(4):
                op("vector", lambda e, h=h: e.tensor_scalar(
                    out=sint[:, 256:512], in0=sint[:, 0:256], scalar1=2 * h,
                    scalar2=3, op0=ALU.logical_shift_right, op1=ALU.bitwise_and),
                    waits=(("LD", ev_pk[it]),) if h == 0 else ())
                op("vector", lambda e: e.tensor_copy(out=sf32[:, 0:256],
                                                     in_=sint[:, 256:512]))
                ev = op("vector", lambda e, it=it, h=h: e.tensor_scalar(
                    out=wt16[:, it * C + 256 * h:it * C + 256 * (h + 1)],
                    in0=sf32[:, 0:256], scalar1=1.0, scalar2=None,
                    op0=ALU.subtract), sem_="DV" if h == 3 else None)
            ev_unp[it] = ev
        ev_bias = dma("sync", sf32[:], d_biast, "LD", waits=(("DV", ev_unp[ND - 1]),))

        ev_vmm = [[None] * ND for _ in range(B)]
        ev_evac = [[None] * ND for _ in range(B)]
        ev_p1 = [[None] * ND for _ in range(B)]
        ev_rmm = [[None] * ND for _ in range(B)]
        ev_rev = [[None] * ND for _ in range(B)]
        ev_pow = [[None] * ND for _ in range(B)]
        ev_corr = [[None] * ND for _ in range(B)]
        ev_ci = [None] * B
        ev_cg = [None] * B
        ev_cya = [None] * B
        ev_rr = [None] * B
        ev_rwkv = [None] * B
        ev_omm = [[[None] * 2 for _ in range(NT)] for _ in range(B)]
        ev_y1 = [[[None] * 2 for _ in range(NT)] for _ in range(B)]
        ev_st1 = [[None] * NT for _ in range(B)]
        ev_ch1 = [None] * B
        ev_xq = [[None] * NT for _ in range(B)]
        ev_x16 = [[None] * NT for _ in range(B)]
        ev_tp = [[None] * ND for _ in range(B)]
        ev_tev = [[None] * ND for _ in range(B)]
        ev_fmm = [[[None] * 2 for _ in range(NT)] for _ in range(B)]
        ev_y2 = [[[None] * 2 for _ in range(NT)] for _ in range(B)]
        ev_st2 = [[None] * NT for _ in range(B)]
        ev_ch2 = [None] * B
        ev_out = [[None] * NT for _ in range(B)]
        ev_st = [[None] * NT for _ in range(B)]
        ev_gx = [16, 0, 0, 0]
        xtk_last = [None]
        out_last = [None, None]

        def nh(b):
            return (b % 2) * ND * TC

        def xh(b):
            return (b % 2) * ND * TC

        psv_last = [None, None]

        def pe_v(b, lo=0, hi=ND, first_extra=()):
            for dt in range(lo, hi):
                _pe_v_dt(b, dt, first_extra if dt == lo else ())
                _act_evacs_dt(b, dt)

        def _pe_v_dt(b, dt, extra):
            if True:
                w = [("GX", ev_gx[b])] + list(extra)
                if b == 0 and dt == 0:
                    w.append(("GW1", 16))
                if psv_last[dt % 2] is not None:
                    w.append(("AC", psv_last[dt % 2]))
                ev = None
                for ct in range(ND):
                    ev = op("tensor", lambda e, b=b, dt=dt, ct=ct: e.matmul(
                        ps_v[dt % 2][:], wv16[:, ct * C + dt * 128:ct * C + (dt + 1) * 128],
                        xfm16[:, xh(b) + ct * TC:xh(b) + (ct + 1) * TC],
                        start=(ct == 0), stop=(ct == ND - 1)),
                        waits=tuple(w) if ct == 0 else (),
                        sem_="PE" if ct == ND - 1 else None)
                ev_vmm[b][dt] = ev

        def pe_r(b):
            for dt in range(ND):
                _pe_r_dt(b, dt)
                _act_rev_dt(b, dt)

        def _pe_r_dt(b, dt):
            if True:
                w = []
                if psv_last[dt % 2] is not None:
                    w.append(("AC", psv_last[dt % 2]))
                if b == 0 and dt == 0:
                    w.append(("GW2", 16))
                ev = None
                for ct in range(ND):
                    ev = op("tensor", lambda e, b=b, dt=dt, ct=ct: e.matmul(
                        ps_v[dt % 2][:], wr16[:, ct * C + dt * 128:ct * C + (dt + 1) * 128],
                        xfm16[:, xh(b) + ct * TC:xh(b) + (ct + 1) * TC],
                        start=(ct == 0), stop=(ct == ND - 1)),
                        waits=tuple(w) if ct == 0 else (),
                        sem_="PE" if ct == ND - 1 else None)
                ev_rmm[b][dt] = ev

        def pe_out(b):
            for tc in range(NT):
                for eh in range(2):
                    w = [("DV", ev_rwkv[b])]
                    if b == 0 and tc == 0 and eh == 0:
                        w.append(("GW3", 16))
                    if tc > 0:
                        w.append(("AC", lambda b=b, tc=tc, eh=eh: ev_y1[b][tc - 1][eh]))
                    elif b > 0:
                        w.append(("AC", ev_y1[b - 1][NT - 1][eh]))
                    ev = None
                    for dt in range(ND):
                        ev = op("tensor", lambda e, dt=dt, tc=tc, eh=eh: e.matmul(
                            ps_o[eh][:], rw16[:, dt * TC + tc * 128:dt * TC + (tc + 1) * 128],
                            wo16[:, dt * C + eh * TC:dt * C + (eh + 1) * TC],
                            start=(dt == 0), stop=(dt == ND - 1)),
                            waits=tuple(w) if dt == 0 else (),
                            sem_="PE" if dt == ND - 1 else None)
                    ev_omm[b][tc][eh] = ev

        def pe_tp(b):
            for it in range(ND):
                _pe_tp_it(b, it)
                _act_tev_it(b, it)

        def _pe_tp_it(b, it):
            if True:
                w_tp = [("DV", ev_x16[b][NT - 1])] if it == 0 else []
                if b == 0 and it == 0:
                    w_tp.append(("LD", ev_consts))
                if it >= 2:
                    w_tp.append(("AC", ev_tev[b][it - 2]))
                elif b > 0:
                    w_tp.append(("AC", ev_tev[b - 1][it + 6]))
                ev = None
                for tc in range(NT):
                    ev = op("tensor", lambda e, it=it, tc=tc: e.transpose(
                        ps_t[it % 2][:, tc * 128:(tc + 1) * 128],
                        rw16[:, tc * C + it * 128:tc * C + (it + 1) * 128], idn[:]),
                        waits=tuple(w_tp) if tc == 0 else (),
                        sem_="PE" if tc == NT - 1 else None)
                ev_tp[b][it] = ev

        def pe_ffn(b):
            for tc in range(NT):
                for oh in range(2):
                    w = [("AC", ev_tev[b][ND - 1])]
                    if b == 0 and tc == 0 and oh == 0:
                        w.append(("DV", ev_unp[ND - 1]))
                        w.append(("LD", ev_bias))
                    if tc > 0:
                        w.append(("DV", lambda b=b, tc=tc, oh=oh: ev_y2[b][tc - 1][oh]))
                    elif b > 0:
                        w.append(("DV", ev_y2[b - 1][NT - 1][oh]))
                    ev = None
                    for it in range(ND):
                        ev = op("tensor", lambda e, it=it, tc=tc, oh=oh: e.matmul(
                            ps_f[oh][:], xqT16[:, it * TC + tc * 128:it * TC + (tc + 1) * 128],
                            wt16[:, it * C + oh * TC:it * C + (oh + 1) * TC],
                            start=(it == 0), stop=(it == ND - 1)),
                            waits=tuple(w) if it == 0 else (),
                            sem_="PE" if it == ND - 1 else None)
                    ev_fmm[b][tc][oh] = ev

        def _act_evacs_dt(b, dt):
            if True:
                w_ev = [("PE", ev_vmm[b][dt])]
                if dt == 0 and b >= 2:
                    w_ev.append(("DV", ev_rwkv[b - 2]))
                if dt < 2 and b > 0:
                    w_ev.append(("DV", ev_p1[b - 1][dt + 6]))
                op("scalar", lambda e, b=b, dt=dt: e.activation(
                    out=num[:, nh(b) + dt * TC:nh(b) + (dt + 1) * TC],
                    in_=ps_v[dt % 2][:], func=AF.Copy,
                    scale=e2[:, dt:dt + 1]), waits=tuple(w_ev))
                w2 = []
                if dt >= 2:
                    w2.append(("DV", lambda b=b, dt=dt: ev_p1[b][dt - 2]))
                ev_evac[b][dt] = op("scalar", lambda e, dt=dt: e.activation(
                    out=scn[:, (dt % 2) * TC:(dt % 2 + 1) * TC], in_=ps_v[dt % 2][:],
                    func=AF.Copy, scale=e12b[:, dt:dt + 1]), waits=tuple(w2))
                psv_last[dt % 2] = ev_evac[b][dt]

        def _act_rev_dt(b, dt):
            if True:
                w = [("PE", ev_rmm[b][dt])]
                if dt == 0 and b > 0:
                    w.append(("PE", ev_tp[b - 1][ND - 1]))
                ev_rev[b][dt] = op("scalar", lambda e, b=b, dt=dt: e.activation(
                    out=rw16[:, dt * TC:(dt + 1) * TC], in_=ps_v[dt % 2][:],
                    func=AF.Tanh, scale=0.5), waits=tuple(w))
                psv_last[dt % 2] = ev_rev[b][dt]

        def act_st1(b):
            for tc in range(NT):
                op("scalar", lambda e, tc=tc: e.activation(
                    out=xqi8[:].bitcast(F32), in_=y1b[:, tc * C:(tc + 1) * C], func=AF.Copy,
                    accum_out=stat[:, tc:tc + 1]), waits=(("XT0", ev_acc[b][tc]),))
                ev_st1[b][tc] = op("scalar", lambda e, tc=tc: e.activation(
                    out=xqi8[:].bitcast(F32), in_=y1b[:, tc * C:(tc + 1) * C], func=AF.Square,
                    accum_out=stat[:, 4 + tc:5 + tc]))

        def act_u(b):
            for tc in range(NT):
                w_xq = [("DV", ev_ch1[b])]
                if tc == 0 and out_last[1] is not None:
                    w_xq.append(("ST0", out_last[1]))
                if tc > 0:
                    w_xq.append(("DV", lambda b=b, tc=tc: ev_xq[b][tc - 1]))
                ev = op("scalar", lambda e, tc=tc: e.activation(
                    out=denb[:], in_=y1b[:, tc * C:(tc + 1) * C],
                    func=AF.Identity, scale=sclq[:, tc:tc + 1], bias=bisq[:, tc:tc + 1]),
                    waits=tuple(w_xq))
                ev_u[b][tc] = ev

        def _act_tev_it(b, it):
            if True:
                w_tev = [("PE", ev_tp[b][it])]
                if b > 0 and it == 0:
                    w_tev.append(("PE", ev_fmm[b - 1][NT - 1][1]))
                ev_tev[b][it] = op("scalar", lambda e, it=it: e.activation(
                    out=xqT16[:, it * TC:(it + 1) * TC], in_=ps_t[it % 2][:],
                    func=AF.Copy), waits=tuple(w_tev))

        def act_st2(b):
            for tc in range(NT):
                op("scalar", lambda e, tc=tc: e.activation(
                    out=xqi8[:].bitcast(F32), in_=y1b[:, tc * C:(tc + 1) * C], func=AF.Copy,
                    accum_out=stat[:, 16 + tc:17 + tc]), waits=(("DV", ev_y2[b][tc][1]),))
                ev_st2[b][tc] = op("scalar", lambda e, tc=tc: e.activation(
                    out=xqi8[:].bitcast(F32), in_=y1b[:, tc * C:(tc + 1) * C], func=AF.Square,
                    accum_out=stat[:, 20 + tc:21 + tc]))

        def act_out(b):
            for tc in range(NT):
                buf = outb if tc % 2 == 0 else denb
                w_o = [("DV", ev_ch2[b])]
                if out_last[tc % 2] is not None:
                    w_o.append(("ST0", out_last[tc % 2]))
                ev_out[b][tc] = op("scalar", lambda e, tc=tc, buf=buf: e.activation(
                    out=buf[:], in_=y1b[:, tc * C:(tc + 1) * C], func=AF.Identity,
                    scale=inv2[:, tc:tc + 1], bias=bis2[:, tc:tc + 1]),
                    waits=tuple(w_o))
                ev_st[b][tc] = dma("sync", d_y[b, tc * 128:(tc + 1) * 128, :],
                                   buf[:], "ST0", waits=(("AC", ev_out[b][tc]),))
                out_last[tc % 2] = ev_st[b][tc]

        def v_p1(b):
            for dt in range(ND):
                w_sc = [("AC", ev_evac[b][dt])]
                if b == 0 and dt == 0:
                    w_sc.append(("DV", ev_post))
                if dt == 0:
                    w_sc.append(("CI", 16 * b))
                h = (dt % 2) * TC
                op("vector", lambda e, dt=dt, h=h: e.tensor_tensor_scan(
                    out=scn[:, h:h + TC],
                    data0=e1b[:, dt:dt + 1].broadcast_to([128, TC]),
                    data1=scn[:, h:h + TC],
                    initial=0.0, op0=ALU.mult, op1=ALU.add), waits=tuple(w_sc))
                op("vector", lambda e, b=b, dt=dt, h=h: e.tensor_add(
                    num[:, nh(b) + dt * TC + 1:nh(b) + (dt + 1) * TC],
                    scn[:, h:h + TC - 1],
                    num[:, nh(b) + dt * TC + 1:nh(b) + (dt + 1) * TC]))
                ev_p1[b][dt] = op("vector", lambda e, dt=dt, h=h: e.tensor_copy(
                    out=scol[:, dt:dt + 1], in_=scn[:, h + TC - 1:h + TC]))

        def v_carry(b):
            op("vector", lambda e: e.tensor_scalar(
                out=cya[:], in0=sg[:, 0:8], scalar1=mskt[:, 0:1], scalar2=None,
                op0=ALU.mult), waits=(("CG", ev_cg[b]),))
            for j in range(1, 8):
                op("vector", lambda e, j=j: e.tensor_mul(cya[:], cya[:], gt[:, j * 8:(j + 1) * 8]))
                op("vector", lambda e, j=j: e.scalar_tensor_tensor(
                    out=cya[:], in0=sg[:, j * 8:(j + 1) * 8], scalar=mskt[:, j:j + 1],
                    in1=cya[:], op0=ALU.mult, op1=ALU.add))
            ev_cya[b] = op("vector", lambda e: e.engine_nop())

        def v_corrwkv(b):
            for dt in range(ND):
                ev_corr[b][dt] = op("vector", lambda e, b=b, dt=dt: e.scalar_tensor_tensor(
                    out=num[:, nh(b) + dt * TC:nh(b) + (dt + 1) * TC],
                    in0=powb[:, dt * TC:(dt + 1) * TC],
                    scalar=cya[:, dt:dt + 1],
                    in1=num[:, nh(b) + dt * TC:nh(b) + (dt + 1) * TC],
                    op0=ALU.mult, op1=ALU.add))
            w_wkv = []
            if b == 0:
                w_wkv.append(("AC", ev_rcpd))
            op("vector", lambda e, b=b: e.tensor_mul(
                num[:, nh(b):nh(b) + ND * TC], num[:, nh(b):nh(b) + ND * TC], rcpd[:]),
                waits=tuple(w_wkv))

        def v_rwkv(b):
            for dt in range(ND):
                ev = op("vector", lambda e, b=b, dt=dt: e.tensor_mul(
                    rw16[:, dt * TC:(dt + 1) * TC],
                    num[:, nh(b) + dt * TC:nh(b) + (dt + 1) * TC],
                    rw16[:, dt * TC:(dt + 1) * TC]),
                    waits=(("GP", ev_rr[b] - ND + 1 + dt),))
            ev_rwkv[b] = ev

        ev_acc = [[None] * NT for _ in range(B)]
        ev_mxmn = [None] * B

        def v_y1(b):
            for tc in range(NT):
                for eh in range(2):
                    w_y1 = [("PE", ev_omm[b][tc][eh])]
                    if b > 0 and eh == 0:
                        w_y1.append(("AC", ev_out[b - 1][tc]))
                    ev_y1[b][tc][eh] = op("scalar", lambda e, b=b, tc=tc, eh=eh: e.activation(
                        out=y1b[:, tc * C + eh * TC:tc * C + (eh + 1) * TC],
                        in_=ps_o[eh][:], func=AF.Copy), waits=tuple(w_y1))
                S.emit(
                    "gpsimd",
                    lambda e, b=b, tc=tc: e.dma_start(
                        out=y1b[:, tc * C:tc * C + TC],
                        in_=d_xtok[b, tc * 128:(tc + 1) * 128, 0:TC],
                        accum_op=ALU.add),
                    waits=(("AC", ev_y1[b][tc][0]),), inc=("XT0", 16))
                ev_acc[b][tc] = S.emit(
                    "gpsimd",
                    lambda e, b=b, tc=tc: e.dma_start(
                        out=y1b[:, tc * C + TC:(tc + 1) * C],
                        in_=d_xtok[b, tc * 128:(tc + 1) * 128, TC:C],
                        accum_op=ALU.add),
                    waits=(("AC", ev_y1[b][tc][1]),), inc=("XT0", 16))
                op("vector", lambda e, tc=tc: e.tensor_reduce(
                    out=stat[:, 8 + tc:9 + tc], in_=y1b[:, tc * C:(tc + 1) * C],
                    axis=AX.X, op=ALU.max), waits=(("XT0", ev_acc[b][tc]),))
                ev = op("vector", lambda e, tc=tc: e.tensor_reduce(
                    out=stat[:, 12 + tc:13 + tc], in_=y1b[:, tc * C:(tc + 1) * C],
                    axis=AX.X, op=ALU.min))
            ev_mxmn[b] = ev

        def v_chain1(b):
            op("vector", lambda e: e.tensor_scalar(
                out=t0c[:], in0=stat[:, 0:4], scalar1=1.0 / C, scalar2=None,
                op0=ALU.mult), waits=(("AC", ev_st1[b][NT - 1]), ("DV", ev_mxmn[b])))
            op("vector", lambda e: e.tensor_mul(t1c[:], t0c[:], t0c[:]))
            op("vector", lambda e: e.scalar_tensor_tensor(
                out=t1c[:], in0=stat[:, 4:8], scalar=1.0 / C, in1=t1c[:],
                op0=ALU.mult, op1=ALU.subtract))
            op("vector", lambda e: e.tensor_scalar(
                out=t1c[:], in0=t1c[:], scalar1=1e-5, scalar2=None, op0=ALU.add))
            op("vector", lambda e: e.tensor_scalar(
                out=inv1[:].bitcast(I32), in0=t1c[:].bitcast(I32), scalar1=1,
                scalar2=None, op0=ALU.logical_shift_right))
            op("vector", lambda e: e.tensor_scalar(
                out=inv1[:].bitcast(I32), in0=inv1[:].bitcast(I32), scalar1=-1,
                scalar2=RSQRT_MAGIC, op0=ALU.mult, op1=ALU.add))
            for _ in range(2):
                op("vector", lambda e: e.tensor_mul(t2c[:], t1c[:], inv1[:]))
                op("vector", lambda e: e.tensor_mul(t2c[:], t2c[:], inv1[:]))
                op("vector", lambda e: e.tensor_scalar(
                    out=t2c[:], in0=t2c[:], scalar1=-0.5, scalar2=1.5,
                    op0=ALU.mult, op1=ALU.add))
                op("vector", lambda e: e.tensor_mul(inv1[:], inv1[:], t2c[:]))
            op("vector", lambda e: e.tensor_sub(t2c[:], stat[:, 8:12], t0c[:]))
            op("vector", lambda e: e.tensor_sub(t1c[:], t0c[:], stat[:, 12:16]))
            op("vector", lambda e: e.tensor_max(t2c[:], t2c[:], t1c[:]))
            op("vector", lambda e: e.tensor_mul(t2c[:], t2c[:], inv1[:]))
            op("vector", lambda e: e.tensor_scalar(
                out=t2c[:], in0=t2c[:], scalar1=1e-5, scalar2=None, op0=ALU.max))
            op("vector", lambda e: e.tensor_scalar(
                out=wsoa[:], in0=t2c[:], scalar1=wsc[:], scalar2=1.0 / 127.0,
                op0=ALU.mult, op1=ALU.mult))
            op("vector", lambda e: e.reciprocal(out=sclq[:], in_=t2c[:]))
            op("vector", lambda e: e.tensor_scalar(
                out=sclq[:], in0=sclq[:], scalar1=127.0, scalar2=None, op0=ALU.mult))
            op("vector", lambda e: e.tensor_mul(sclq[:], sclq[:], inv1[:]))
            op("vector", lambda e: e.tensor_mul(bisq[:], t0c[:], sclq[:]))
            ev_ch1[b] = op("vector", lambda e: e.tensor_scalar(
                out=bisq[:], in0=bisq[:], scalar1=-1.0, scalar2=None, op0=ALU.mult))

        def v_quant(b):
            for tc in range(NT):
                ev_xq[b][tc] = op("vector", lambda e, tc=tc: e.tensor_scalar(
                    out=xqi8[:, tc * C:(tc + 1) * C], in0=denb[:], scalar1=12582912.0,
                    scalar2=-12582912.0, op0=ALU.add, op1=ALU.add),
                    waits=(("AC", ev_u[b][tc]),))

        def v_y2(b):
            for tc in range(NT):
                for oh in range(2):
                    op("vector", lambda e, b=b, tc=tc, oh=oh: e.scalar_tensor_tensor(
                        out=denb[:, oh * TC:(oh + 1) * TC], in0=ps_f[oh][:],
                        scalar=wsoa[:, tc:tc + 1], in1=sf32[:, oh * TC:(oh + 1) * TC],
                        op0=ALU.mult, op1=ALU.add),
                        waits=(("PE", ev_fmm[b][tc][oh]), ("DV", ev_xq[b][NT - 1])))
                    ev_y2[b][tc][oh] = op("vector", lambda e, tc=tc, oh=oh: e.scalar_tensor_tensor(
                        out=y1b[:, tc * C + oh * TC:tc * C + (oh + 1) * TC],
                        in0=y1b[:, tc * C + oh * TC:tc * C + (oh + 1) * TC],
                        scalar=inv1[:, tc:tc + 1],
                        in1=denb[:, oh * TC:(oh + 1) * TC],
                        op0=ALU.mult, op1=ALU.add))

        def v_chain2(b):
            op("vector", lambda e: e.tensor_scalar(
                out=t0c[:], in0=stat[:, 16:20], scalar1=1.0 / C, scalar2=None,
                op0=ALU.mult), waits=(("AC", ev_st2[b][NT - 1]),))
            op("vector", lambda e: e.tensor_mul(t1c[:], t0c[:], t0c[:]))
            op("vector", lambda e: e.scalar_tensor_tensor(
                out=t1c[:], in0=stat[:, 20:24], scalar=1.0 / C, in1=t1c[:],
                op0=ALU.mult, op1=ALU.subtract))
            op("vector", lambda e: e.tensor_scalar(
                out=t1c[:], in0=t1c[:], scalar1=1e-5, scalar2=None, op0=ALU.add))
            op("vector", lambda e: e.tensor_scalar(
                out=inv2[:].bitcast(I32), in0=t1c[:].bitcast(I32), scalar1=1,
                scalar2=None, op0=ALU.logical_shift_right))
            op("vector", lambda e: e.tensor_scalar(
                out=inv2[:].bitcast(I32), in0=inv2[:].bitcast(I32), scalar1=-1,
                scalar2=RSQRT_MAGIC, op0=ALU.mult, op1=ALU.add))
            for _ in range(3):
                op("vector", lambda e: e.tensor_mul(t2c[:], t1c[:], inv2[:]))
                op("vector", lambda e: e.tensor_mul(t2c[:], t2c[:], inv2[:]))
                op("vector", lambda e: e.tensor_scalar(
                    out=t2c[:], in0=t2c[:], scalar1=-0.5, scalar2=1.5,
                    op0=ALU.mult, op1=ALU.add))
                op("vector", lambda e: e.tensor_mul(inv2[:], inv2[:], t2c[:]))
            op("vector", lambda e: e.tensor_mul(bis2[:], t0c[:], inv2[:]))
            ev_ch2[b] = op("vector", lambda e: e.tensor_scalar(
                out=bis2[:], in0=bis2[:], scalar1=-1.0, scalar2=None, op0=ALU.mult))

        def g_gather(b):
            ev_ci[b] = dma("sync", d_ccin[b].rearrange("(p f) -> p f", p=128), scol[:],
                           "CI", waits=(("DV", ev_p1[b][ND - 1]),))
            S.emit("gpsimd", lambda e, b=b: e.collective_compute(
                "AllGather", ALU.bypass, ins=[d_ccin[b]], outs=[d_ccout[b]],
                replica_groups=[[0, 1, 2, 3, 4, 5, 6, 7]]),
                waits=(("CI", ev_ci[b]),), inc=("CC", 1))
            w_sg = [("CC", b + 1)]
            if b > 0:
                w_sg.append(("DV", ev_cya[b - 1]))
            ev_cg[b] = dma("gpsimd", sg[:].rearrange("p (j f) -> p j f", j=8),
                           d_ccout[b].rearrange("j (p f) -> p j f", p=128), "CG",
                           waits=tuple(w_sg))

        def g_rr(b):
            for dt in range(ND):
                ev_rr[b] = S.emit("gpsimd", lambda e, dt=dt: e.tensor_scalar(
                    out=rw16[:, dt * TC:(dt + 1) * TC], in0=rw16[:, dt * TC:(dt + 1) * TC],
                    scalar1=0.5, scalar2=0.5, op0=ALU.mult, op1=ALU.add),
                    waits=(("AC", ev_rev[b][dt]),))

        def v_x16(b):
            for tc in range(NT):
                w = []
                if tc == 0:
                    w.append(("PE", ev_omm[b][NT - 1][1]))
                ev_x16[b][tc] = op("vector", lambda e, tc=tc: e.tensor_copy(
                    out=rw16[:, tc * C:(tc + 1) * C], in_=xqi8[:, tc * C:(tc + 1) * C]),
                    waits=tuple(w))

        def g_xfm(b):
            ev_gx[b] = dma("gpsimd", xfm16[:, xh(b):xh(b) + ND * TC].rearrange(
                "p (ct t) -> p ct t", ct=ND),
                d_xfm[b].rearrange("(ct p) t -> p ct t", p=128), "GX",
                waits=(("PE", ev_rmm[b - 2][ND - 1]),))

        ev_u = [[None] * NT for _ in range(B)]

        pe_v(0)
        v_p1(0)
        g_gather(0)
        pe_r(0)
        g_rr(0)
        ev_gx[1] = dma("gpsimd", xfm16[:, ND * TC:2 * ND * TC].rearrange(
            "p (ct t) -> p ct t", ct=ND),
            d_xfm[1].rearrange("(ct p) t -> p ct t", p=128), "GX")
        pe_v(1)
        v_carry(0)

        v_corrwkv(0)
        for b in range(B):
            v_rwkv(b)
            if b > 0:
                for kv in range(ev_y2[b - 1][0][0], ev_rwkv[b], 4):
                    op("tensor", lambda e: e.matmul(
                        ps_o[0][:, 0:8], idn[:], idn[:, 0:8],
                        start=True, stop=True), waits=(("DV", kv),))
            pe_out(b)
            if b + 2 < B:
                g_xfm(b + 2)
            if b + 1 < B:
                v_p1(b + 1)
            v_y1(b)
            if b + 2 < B:
                pe_v(b + 2, 0, 2)
            act_st1(b)
            v_chain1(b)
            act_u(b)
            v_quant(b)
            v_x16(b)
            if b + 1 < B:
                g_gather(b + 1)
                v_carry(b + 1)
                v_corrwkv(b + 1)
            w0 = [("AC", ev_y1[b][NT - 1][0])]
            if b == 0:
                w0.append(("LD", ev_consts))
            for kv in range(ev_mxmn[b] + 2, ev_x16[b][NT - 1], 3):
                op("tensor", lambda e: e.matmul(
                    ps_o[0][:, 0:8], idn[:], idn[:, 0:8],
                    start=True, stop=True), waits=tuple([("DV", kv)] + w0))
                w0 = []
            pe_tp(b)
            if b + 1 < B:
                pe_r(b + 1)
                g_rr(b + 1)
            pe_ffn(b)
            v_y2(b)
            act_st2(b)
            v_chain2(b)
            act_out(b)
            if b + 2 < B:
                pe_v(b + 2, 2, ND)

        S.emit("sync", None, waits=(("ST0", S.cnt["ST0"]),))

        S.check()

        with nc.Block() as block:
            def runner(eng_name):
                def body(eng):
                    for fn, waits, inc in S.ops[eng_name]:
                        for s, v in waits:
                            eng.wait_ge(sems[s], v() if callable(v) else v)
                        if fn is not None:
                            instr = fn(eng)
                            if inc is not None:
                                instr.then_inc(sems[inc[0]], inc[1])
                return body
            block.sync(runner("sync"))
            block.scalar(runner("scalar"))
            block.vector(runner("vector"))
            block.tensor(runner("tensor"))
            block.gpsimd(runner("gpsimd"))

    return nc


_NC_CACHE = None


def kernel(L0_attn_x, L0_attn_time_decay, L0_attn_time_first, L0_attn_key_w,
           L0_attn_value_w, L0_attn_recep_w, L0_attn_out_w, L0_ffn_packed_weight,
           L0_ffn_weight_scale, L0_ffn_bias):
    global _NC_CACHE
    x = np.asarray(L0_attn_x, dtype=np.float32)
    td = np.asarray(L0_attn_time_decay, dtype=np.float32)
    tf = np.asarray(L0_attn_time_first, dtype=np.float32)
    wv = np.asarray(L0_attn_value_w, dtype=np.float32)
    wr = np.asarray(L0_attn_recep_w, dtype=np.float32)
    wo = np.asarray(L0_attn_out_w, dtype=np.float32)
    pk = np.asarray(L0_ffn_packed_weight, dtype=np.int32)
    ws = np.asarray(L0_ffn_weight_scale, dtype=np.float32)
    bi = np.asarray(L0_ffn_bias, dtype=np.float32)

    wvT = np.ascontiguousarray(wv.T)
    wrT = np.ascontiguousarray(wr.T)
    woT = np.ascontiguousarray(wo.T)
    pkT = np.ascontiguousarray(pk.reshape(256, C).T)
    tdt = np.ascontiguousarray(td.reshape(ND, 128).T)
    tft = np.ascontiguousarray(tf.reshape(ND, 128).T)
    iota = np.ascontiguousarray(np.broadcast_to(
        np.arange(TC, dtype=np.float32), (128, TC)))
    wscol = np.ascontiguousarray(np.broadcast_to(ws.reshape(1, 1), (128, 1)))
    ident = np.eye(128, dtype=ml_dtypes.bfloat16)
    biast = np.ascontiguousarray(np.broadcast_to(bi.reshape(1, C), (128, C)))

    in_maps = []
    for k in range(NCORES):
        sl = x[:, TC * k:TC * (k + 1), :]
        maskt = np.ascontiguousarray(np.broadcast_to(
            (np.arange(8) < k).astype(np.float32), (128, 8)))
        gsel = np.ascontiguousarray(np.broadcast_to(
            np.repeat((np.arange(8) < k).astype(np.float32), 8), (128, 64)))
        in_maps.append(dict(
            xfm=np.ascontiguousarray(sl.transpose(0, 2, 1)),
            xtok=np.ascontiguousarray(sl),
            wvT=wvT, wrT=wrT, woT=woT, pkT=pkT, tdt=tdt, tft=tft, iota=iota,
            maskt=maskt, gsel=gsel, gselc=np.ascontiguousarray(1.0 - gsel),
            wscol=wscol, ident=ident, biast=biast,
        ))

    if _NC_CACHE is None:
        _NC_CACHE = build_nc()
    res = run_bass_kernel_spmd(_NC_CACHE, in_maps, core_ids=list(range(NCORES)))
    out = np.empty((B, NCORES * TC, C), dtype=np.float32)
    for k in range(NCORES):
        out[:, TC * k:TC * (k + 1), :] = res.results[k]["y"]
    return out
